# version 02
# baseline (speedup 1.0000x reference)
import contextlib
import numpy as np
import ml_dtypes

import concourse.bass as bass
import concourse.mybir as mybir
from concourse.bass_utils import run_bass_kernel_spmd

F32 = mybir.dt.float32
BF16 = mybir.dt.bfloat16
I32 = mybir.dt.int32
ALU = mybir.AluOpType
AF = mybir.ActivationFunctionType
AX = mybir.AxisListType
bf = ml_dtypes.bfloat16

B, S, D = 4, 2048, 2048
NC = 8
TLOC = 1024
NCH = D // 128
TT = 512
NTT = TLOC // TT
H, KVH, HD, W = 16, 4, 128, 512
HID2, HID = 16384, 8192
NHC = HID // 128
SCALE = float(1.0 / np.sqrt(HD))
EPS_LN = 1e-5
EPS_ATTN = 1e-6
EPS_F32 = float(np.finfo(np.float32).eps)
NQT = TLOC // 128
KWIN = 640
DDG = [(0, 6), (6, 6), (12, 4)]

ENGINES = ["sync", "gpsimd", "vector", "scalar", "tensor"]


class Prog:

    def __init__(self, nc, ctx):
        self.nc = nc
        self.ctx = ctx
        self.q = {e: [] for e in ENGINES}
        self.sems = {}
        self.counts = {}
        self.waited = {e: {} for e in ENGINES}

    def sem(self, name):
        self.sems[name] = self.ctx.enter_context(self.nc.semaphore(name))
        self.counts[name] = 0
        return name

    def emit(self, eng, fn, waits=(), inc=None):
        waits = [t for t in waits if t is not None]
        waits = [(s, v) for (s, v) in waits if v > 0 and self.waited[eng].get(s, 0) < v]
        for s, v in waits:
            assert v <= self.counts[s], f"wait on future token {s}={v} (now {self.counts[s]})"
            self.waited[eng][s] = v
        if inc is None:
            incs = []
        elif isinstance(inc, str):
            incs = [(inc, 1)]
        else:
            incs = list(inc)
        for s, by in incs:
            self.counts[s] += by
        sems = self.sems
        carried = eng in ("vector", "scalar") and any(s != "fen" for s, _ in incs)
        prog = self

        def run(engine, fn=fn, waits=waits, incs=incs, carried=carried, eng=eng):
            for s, v in waits:
                engine.wait_ge(sems[s], v)
            ins = fn(engine)
            if carried:
                car = engine.drain(fusable=False)
                for s, by in incs:
                    car = car.then_inc(sems[s], by)
            else:
                for s, by in incs:
                    ins = ins.then_inc(sems[s], by)

        self.q[eng].append(run)

    def tok(self, name):
        return (name, self.counts[name])

    def wait(self, eng, *toks):
        self.emit(eng, lambda e: None, waits=list(toks))

    def efence(self, eng, fn, waits=()):
        self.emit(eng, fn, waits=waits, inc=self.FEN)
        self.emit(eng, lambda e: None, waits=[self.tok(self.FEN)])

    def dma(self, eng, out, in_, waits=(), sem=None):
        sem = sem or (self.GLD if eng == "gpsimd" else self.WLD)
        self.emit(eng, lambda e, o=out, i=in_: e.dma_start(out=o, in_=i),
                  waits=waits, inc=[(sem, 16)])
        return self.tok(sem)

    def replay(self, block):
        for name in ENGINES:
            fns = self.q[name]
            getattr(block, name)(lambda engine, fns=fns: [f(engine) for f in fns])


def build_program(parts=("ffn1", "attn", "ffn2")):
    nc = bass.Bass("TRN2", target_bir_lowering=False, debug=False)

    def din(name, shp, dt=F32):
        return nc.dram_tensor(name, shp, dt, kind="ExternalInput").ap()

    x0T = din("x0T", [D, TLOC])
    wq_in = din("wq", [D, D])
    wk_in = din("wk", [D, KVH * HD])
    wv_in = din("wv", [D, KVH * HD])
    wo_in = din("wo", [D, D])
    gu_raw = din("gu_raw", [HID2 // NC, D])
    wp_raw = din("wp_raw", [D, HID // NC])
    lnw_in = din("lnw", [128, NCH])
    ident_in = din("ident", [128, 128], BF16)
    ones_in = din("onesf", [1, 128])
    onesb_in = din("onesb", [128, 1], BF16)
    ropeC_in = din("ropeC", [128, 1024], BF16)
    ropeS_in = din("ropeS", [128, 1024], BF16)
    pswap_in = din("pswap", [128, 128])
    mask_in = din("maskmul", [NQT, 128, KWIN], BF16)
    off_k_in = din("off_k", [1, 1], I32)
    off_v_in = din("off_v", [1, 1], I32)

    out_d = nc.dram_tensor("out", [D, TLOC], F32, kind="ExternalOutput").ap()
    dbg = "dbg" in parts
    if dbg:
        dbg_f = nc.dram_tensor("dbg_f", [128, D], BF16, kind="ExternalOutput").ap()
        dbg_fwp = nc.dram_tensor("dbg_fwp", [128, 2 * 1024], BF16, kind="ExternalOutput").ap()
        dbg_st2 = nc.dram_tensor("dbg_st2", [128, D], BF16, kind="ExternalOutput").ap()
        dbg_xn = nc.dram_tensor("dbg_xn", [128, NCH * TLOC], BF16, kind="ExternalOutput").ap()
        dbg_kT = nc.dram_tensor("dbg_kT", [128, KVH * 1536], BF16, kind="ExternalOutput").ap()
        dbg_qv = nc.dram_tensor("dbg_qv", [128, TLOC], BF16, kind="ExternalOutput").ap()
        dbg_ao = nc.dram_tensor("dbg_ao", [128, NCH * TLOC], BF16, kind="ExternalOutput").ap()
        dbg_stage = nc.dram_tensor("dbg_stage", [128, D], BF16, kind="ExternalOutput").ap()
        dbg_scr = nc.dram_tensor("dbg_scr", [128, D], BF16, kind="ExternalOutput").ap()
        dbg_gu = nc.dram_tensor("dbg_gu", [512, D], BF16, kind="ExternalOutput").ap()
        dbg_p = nc.dram_tensor("dbg_p", [256, D], BF16, kind="ExternalOutput").ap()
        dbg_xln = nc.dram_tensor("dbg_xln", [128, NCH * TT], BF16, kind="ExternalOutput").ap()
        dbg_h = nc.dram_tensor("dbg_h", [128, 4 * TT], BF16, kind="ExternalOutput").ap()

    ag_gu_in = nc.dram_tensor("ag_gu_in", [HID2 // NC, D], BF16, kind="Internal").ap()
    ag_gu = nc.dram_tensor("ag_gu", [HID2, D], BF16, kind="Internal").ap()
    ag_p_in = nc.dram_tensor("ag_p_in", [HID // NC, D], BF16, kind="Internal").ap()
    ag_p = nc.dram_tensor("ag_p", [HID, D], BF16, kind="Internal").ap()
    kv_in = nc.dram_tensor("kv_in", [1024, 512], BF16, kind="Internal").ap()
    kv_out = nc.dram_tensor("kv_out", [NC * 1024, 512], BF16, kind="Internal").ap()

    with contextlib.ExitStack() as ctx:
        sb = lambda nm, shp, dt: ctx.enter_context(nc.sbuf_tensor(nm, shp, dt))

        xT = sb("xT", [128, NCH * TLOC], F32)
        arena = sb("arena", [128, 32768], BF16)
        scr16 = sb("scr16", [128, 8192], BF16)
        rows = sb("rows", [1, 6 * TT], F32)
        bcA = sb("bcA", [128, TT], F32)
        bcB = sb("bcB", [128, TT], F32)
        rtmp = sb("rtmp", [128, TT], F32)
        lnw = sb("lnw_t", [128, NCH], F32)
        ident = sb("ident_t", [128, 128], BF16)
        onesf = sb("onesf_t", [1, 128], F32)
        onesb = sb("onesb_t", [128, 1], BF16)
        offk_sb = sb("offk_sb", [1, 1], I32)
        offv_sb = sb("offv_sb", [1, 1], I32)
        xb_s = [sb(f"xb{i}", [128, TT], BF16) for i in range(2)]
        sq_s = [sb(f"sq{i}", [128, TT], BF16) for i in range(2)]

        dumt = sb("dumt", [1, 16], F32)
        dumb = sb("dumb", [1, 16], BF16)
        p = Prog(nc, ctx)
        p.dum = dumt

        def mk_pe_mark(ps_mark):
            def pe_mark(waits=(), inc=None):
                p.emit("tensor", lambda e: e.matmul(ps_mark[0:1, 0:1], dumb[0:1, 0:1],
                                                    dumb[0:1, 8:9], start=True, stop=True),
                       waits=waits, inc=inc)
            return pe_mark
        WLD = p.sem("wld"); p.WLD = WLD
        GLD = p.sem("gld"); p.GLD = GLD
        FEN = p.sem("fen"); p.FEN = FEN
        DV = p.sem("dv")
        AC = p.sem("ac")
        PE = p.sem("pe")
        RMS = p.sem("rms")
        CC = p.sem("cc")
        ST = p.sem("st")
        WFG = p.sem("wfg")
        WFU = p.sem("wfu")
        WFP = p.sem("wfp")
        WFA = p.sem("wfa")
        QF = p.sem("qf")

        t_id = p.dma("sync", ident[:], ident_in[:])
        t_x = p.dma("sync", xT[:].rearrange("p (c t) -> p c t", t=TLOC),
                    x0T.rearrange("(c p) t -> p c t", p=128))
        t_lnw = p.dma("sync", lnw[:], lnw_in[:])
        t_o1 = p.dma("sync", onesf[:], ones_in[:])
        t_ob = p.dma("sync", onesb[:], onesb_in[:])
        t_offk = p.dma("sync", offk_sb[:], off_k_in[:])
        t_offv = p.dma("sync", offv_sb[:], off_v_in[:])
        t_consts = t_offv

        def av_(off, ncol):
            return arena[:, off:off + ncol]

        qw = [av_(0 + 2048 * i, 2048) for i in range(2)]
        qf = [av_(4096 + 2048 * i, 2048) for i in range(2)]
        qstage = [av_(8192 + 2048 * i, 2048) for i in range(2)]
        qrq = av_(12288, 2048)
        fwp = av_(16384, 16384)
        with contextlib.ExitStack() as qctx:
            qsb = lambda nm, shp, dt: qctx.enter_context(nc.sbuf_tensor(nm, shp, dt))
            qps = lambda nm, shp, dt: qctx.enter_context(nc.psum_tensor(nm, shp, dt))
            ps_mark_q = qctx.enter_context(nc.psum_tensor("ps_mark_q", [1, 8], F32))
            pe_mark = mk_pe_mark(ps_mark_q)
            qgs = qsb("qgs", [128, 32], F32)
            qscb = qsb("qscb", [128, 32], BF16)
            qscf = qsb("qscf", [128, 32], F32)
            qrcp = qsb("qrcp", [128, 32], F32)
            ps_tp = [qps(f"ps_tp{i}", [128, 2048], BF16) for i in range(2)]

            evac_toks = []
            store_toks = []
            tp_toks = []

            def quant_core(idx, wbuf, dst, ncols, ngrp, t_ld):
                v = "vector"
                p.efence(v, lambda e: e.tensor_reduce(
                    qgs[:, 0:ngrp], wbuf.rearrange("p (g k) -> p g k", k=64),
                    axis=AX.X, op=ALU.add, apply_absolute_value=True), waits=[t_ld])
                p.efence(v, lambda e: e.tensor_scalar(qscb[:, 0:ngrp], qgs[:, 0:ngrp],
                                                      1.0 / 64, 1e-8, ALU.mult, ALU.max))
                p.efence(v, lambda e: e.tensor_copy(qscf[:, 0:ngrp], qscb[:, 0:ngrp]))
                p.efence(v, lambda e: e.reciprocal(qrcp[:, 0:ngrp], qscf[:, 0:ngrp]))
                qr = qrq[:, 0:ncols]
                p.emit(v, lambda e: e.tensor_tensor(
                    qr.rearrange("p (g k) -> p g k", k=64), wbuf,
                    qrcp[:, 0:ngrp, None].broadcast_to([128, ngrp, 64]), ALU.mult))
                p.emit(v, lambda e: e.tensor_scalar_add(qr, qr, 192.0))
                p.emit(v, lambda e: e.tensor_scalar(qr, qr, 193.0, 191.0, ALU.min, ALU.max))
                p.emit(v, lambda e: e.tensor_scalar_add(qr, qr, -192.0))
                p.emit(v, lambda e: e.tensor_tensor(
                    dst.rearrange("p (g k) -> p g k", k=64), qr,
                    qscb[:, 0:ngrp, None].broadcast_to([128, ngrp, 64]), ALU.mult))
                p.emit(v, lambda e: e.tensor_sub(qr, dst, wbuf))
                p.emit(v, lambda e: e.tensor_add(dst, wbuf, qr), inc=QF)
                return p.tok(QF)

            for ddc in range(16):
                par = ddc % 2
                wbuf = qw[par]
                waits_ld = []
                if ddc >= 2:
                    waits_ld.append((QF, p.counts[QF] - 1))
                t_ld = p.dma("gpsimd", wbuf[:, 0:1024],
                             wp_raw[ddc * 128:(ddc + 1) * 128, :],
                             waits=waits_ld)
                quant_core(ddc, wbuf[:, 0:1024],
                           fwp[:, ddc * 1024:(ddc + 1) * 1024], 1024, 16, t_ld)
            t_fwp = p.tok(QF)
            if dbg:
                p.dma("sync", dbg_fwp[:], fwp[:, 0:2048], waits=[t_fwp], sem=ST)
            wstg = 0
            for hs in range(8):
                par = hs % 2
                stage, pst = qstage[par], ps_tp[par]
                wtp = [t_fwp]
                if len(evac_toks) >= 2:
                    wtp.append(evac_toks[-2])
                for ddc in range(16):
                    p.emit("tensor", lambda e, ddc=ddc, hs=hs, pst=pst: e.transpose(
                        pst[:, ddc * 128:(ddc + 1) * 128],
                        fwp[:, ddc * 1024 + hs * 128: ddc * 1024 + (hs + 1) * 128],
                        ident[:]), waits=wtp if ddc == 0 else [])
                pe_mark(inc=PE)
                t_tp = p.tok(PE)
                wev = [t_tp]
                if len(store_toks) >= 2:
                    wev.append(store_toks[-2])
                p.emit("scalar", lambda e, stage=stage, pst=pst: e.copy(stage, pst[:]),
                       waits=wev, inc=AC)
                evac_toks.append(p.tok(AC))
                if dbg and hs == 0:
                    p.dma("sync", dbg_st2[:], stage, waits=[evac_toks[-1]], sem=ST)
                store_toks.append(
                    p.dma("gpsimd", ag_p_in[hs * 128:(hs + 1) * 128, :], stage,
                          waits=[evac_toks[-1]]))
                wstg += 1
            t_p_written = p.tok(GLD)
            p.emit("gpsimd", lambda e: e.collective_compute(
                "AllGather", ALU.bypass, replica_groups=[list(range(NC))],
                ins=[ag_p_in[:]], outs=[ag_p[:]]), waits=[t_p_written], inc=CC)
            T_AG_P = p.tok(CC)
            for rc in range(16):
                par = rc % 2
                wbuf, fbuf, stage, pst = qw[par], qf[par], qstage[par], ps_tp[par]
                waits_ld = [(QF, p.counts[QF] - 1)]
                t_ld = p.dma("gpsimd", wbuf, gu_raw[rc * 128:(rc + 1) * 128, :],
                             waits=waits_ld)
                if rc >= 2:
                    p.wait("vector", tp_toks[rc - 2])
                t_f = quant_core(rc, wbuf, fbuf, 2048, 32, t_ld)
                if dbg and rc == 0:
                    p.dma("sync", dbg_f[:], fbuf, waits=[t_f], sem=ST)
                wtp = [t_f, t_id]
                if len(evac_toks) >= 2:
                    wtp.append(evac_toks[-2])
                for s in range(16):
                    p.emit("tensor", lambda e, s=s, fbuf=fbuf, pst=pst: e.transpose(
                        pst[:, s * 128:(s + 1) * 128], fbuf[:, s * 128:(s + 1) * 128],
                        ident[:]), waits=wtp if s == 0 else [])
                pe_mark(inc=PE)
                tp_toks.append(p.tok(PE))
                wev = [tp_toks[-1]]
                if len(store_toks) >= 2:
                    wev.append(store_toks[-2])
                p.emit("scalar", lambda e, stage=stage, pst=pst: e.copy(stage, pst[:]),
                       waits=wev, inc=AC)
                evac_toks.append(p.tok(AC))
                if dbg and rc == 0:
                    p.dma("sync", dbg_stage[:], stage, waits=[evac_toks[-1]], sem=ST)
                store_toks.append(
                    p.dma("gpsimd", ag_gu_in[rc * 128:(rc + 1) * 128, :], stage,
                          waits=[evac_toks[-1]] + ([p.tok(ST)] if (dbg and rc == 0) else [])))

            t_gu_written = p.tok(GLD)
            p.emit("gpsimd", lambda e: e.collective_compute(
                "AllGather", ALU.bypass, replica_groups=[list(range(NC))],
                ins=[ag_gu_in[:]], outs=[ag_gu[:]]), waits=[t_gu_written], inc=CC)
            T_AG_GU = p.tok(CC)

            t_quant_ac = evac_toks[-1]

        def ln_block(ps_A, ps_B, tt0, pre_pe):
            stat_toks = []
            for c in range(NCH):
                par = c % 2
                xb, sq = xb_s[par], sq_s[par]
                w0 = [t_x]
                if c >= 2:
                    w0.append(stat_toks[c - 2])
                p.emit("vector", lambda e, c=c, xb=xb: e.tensor_copy(
                    xb[:], xT[:, c * TLOC + tt0: c * TLOC + tt0 + TT]), waits=w0)
                p.emit("vector", lambda e, xb=xb, sq=sq: e.tensor_mul(sq[:], xb[:], xb[:]),
                       inc=DV)
                tdv = p.tok(DV)
                p.emit("tensor", lambda e, c=c, xb=xb: e.matmul(
                    ps_A[0:1, :], onesb[:], xb[:], start=(c == 0), stop=(c == NCH - 1)),
                    waits=[tdv, t_ob] + (pre_pe if c == 0 else []))
                p.emit("tensor", lambda e, c=c, sq=sq: e.matmul(
                    ps_B[0:1, :], onesb[:], sq[:], start=(c == 0), stop=(c == NCH - 1)),
                    inc=PE)
                stat_toks.append(p.tok(PE))
            t_stat = stat_toks[-1]
            v = "vector"
            p.emit(v, lambda e: e.tensor_copy(rows[0:1, 0:TT], ps_A[0:1, :]), waits=[t_stat])
            p.efence(v, lambda e: e.tensor_copy(rows[0:1, TT:2 * TT], ps_B[0:1, :]))
            p.emit(v, lambda e: e.tensor_scalar_mul(rows[0:1, 2 * TT:3 * TT], rows[0:1, 0:TT], 1.0 / D))
            p.efence(v, lambda e: e.tensor_scalar_mul(rows[0:1, 3 * TT:4 * TT], rows[0:1, TT:2 * TT], 1.0 / D))
            p.efence(v, lambda e: e.tensor_mul(rows[0:1, 4 * TT:5 * TT],
                                               rows[0:1, 2 * TT:3 * TT], rows[0:1, 2 * TT:3 * TT]))
            p.emit(v, lambda e: e.scalar_tensor_tensor(
                rows[0:1, 3 * TT:4 * TT], rows[0:1, 3 * TT:4 * TT], EPS_LN,
                rows[0:1, 4 * TT:5 * TT], ALU.add, ALU.subtract), inc=DV)
            t_var = p.tok(DV)
            p.emit("scalar", lambda e: e.activation(rows[0:1, 4 * TT:5 * TT],
                                                    rows[0:1, 3 * TT:4 * TT], AF.Ln,
                                                    bias=0.0, scale=1.0),
                   waits=[t_var], inc=AC)
            t_ln = p.tok(AC)
            p.emit(v, lambda e: e.tensor_scalar_mul(rows[0:1, 4 * TT:5 * TT],
                                                    rows[0:1, 4 * TT:5 * TT], -0.5),
                   waits=[t_ln], inc=DV)
            t_nl = p.tok(DV)
            p.emit("scalar", lambda e: e.activation(rows[0:1, 4 * TT:5 * TT],
                                                    rows[0:1, 4 * TT:5 * TT], AF.Exp,
                                                    bias=0.0, scale=1.0),
                   waits=[t_nl], inc=AC)
            t_rstd = p.tok(AC)
            p.emit(v, lambda e: e.scalar_tensor_tensor(
                rows[0:1, 5 * TT:6 * TT], rows[0:1, 2 * TT:3 * TT], -1.0,
                rows[0:1, 4 * TT:5 * TT], ALU.mult, ALU.mult), waits=[t_rstd], inc=DV)
            t_ab = p.tok(DV)
            p.emit("tensor", lambda e: e.matmul(ps_A[:, :], onesf[:], rows[0:1, 4 * TT:5 * TT],
                                                start=True, stop=True), waits=[t_ab, t_o1])
            p.emit("tensor", lambda e: e.matmul(ps_B[:, :], onesf[:], rows[0:1, 5 * TT:6 * TT],
                                                start=True, stop=True), inc=PE)
            t_bc = p.tok(PE)
            p.emit("scalar", lambda e: e.copy(bcA[:], ps_A[:, :]), waits=[t_bc])
            p.emit("scalar", lambda e: e.copy(bcB[:], ps_B[:, :]), inc=AC)
            return p.tok(AC)

        def ffn_call(tag, pre_pe_tok):
            hbuf = av_(0, NHC * TT)
            with contextlib.ExitStack() as fctx:
                fsb = lambda nm, shp, dt: fctx.enter_context(nc.sbuf_tensor(nm, shp, dt))
                fps = lambda nm, shp, dt: fctx.enter_context(nc.psum_tensor(nm, shp, dt))
                xln = scr16[:, 0:NCH * TT]
                wg_b = [fsb(f"wgb{tag}{i}", [128, 2048], BF16) for i in range(2)]
                wu_b = [fsb(f"wub{tag}{i}", [128, 2048], BF16) for i in range(2)]
                wp_b = [fsb(f"wpb{tag}{i}", [128, 2 * 768], BF16) for i in range(2)]
                sg_s = [fsb(f"sg{tag}{i}", [128, TT], BF16) for i in range(2)]
                ue_s = [fsb(f"ue{tag}{i}", [128, TT], BF16) for i in range(2)]
                binv = fsb(f"binv_{tag}", [128, TT], F32)
                ps_A = fps(f"psA{tag}", [128, TT], F32)
                ps_B = fps(f"psB{tag}", [128, TT], F32)
                ps_bank = [fps(f"psk{tag}{i}", [128, TT], F32) for i in range(6)]

                pre = [pre_pe_tok]
                t_tile_guard = pre_pe_tok
                for tt in range(NTT):
                    tt0 = tt * TT
                    t_bcs = ln_block(ps_A, ps_B, tt0, pre)
                    pre = []
                    for c in range(NCH):
                        dst = xln[:, c * TT:(c + 1) * TT]
                        xsl = xT[:, c * TLOC + tt0: c * TLOC + tt0 + TT]
                        p.emit("vector", lambda e, dst=dst, xsl=xsl: e.tensor_mul(dst, xsl, bcA[:]),
                               waits=[t_bcs])
                        p.emit("vector", lambda e, dst=dst: e.tensor_add(dst, dst, bcB[:]),
                               inc=(DV if c == NCH - 1 else None))
                    t_xln = p.tok(DV)

                    gate_toks = {}
                    up_toks = {}

                    def load_gu(hc):
                        wg = [T_AG_GU, (WFG, p.counts[WFG])]
                        wu = [(WFU, p.counts[WFU])]
                        if hc < 2 and t_tile_guard is not None:
                            wg.append(t_tile_guard)
                            wu.append(t_tile_guard)
                        gate_toks[hc] = p.dma(
                            "sync", wg_b[hc % 2][:],
                            ag_gu[hc * 128:(hc + 1) * 128, :], waits=wg, sem=WLD)
                        up_toks[hc] = p.dma(
                            "sync", wu_b[hc % 2][:],
                            ag_gu[(64 + hc) * 128:(65 + hc) * 128, :], waits=wu, sem=WLD)

                    load_gu(0)
                    load_gu(1)
                    for hc in range(NHC):
                        par = hc % 2
                        pg, pu = ps_bank[2 * par], ps_bank[2 * par + 1]
                        wgu = [gate_toks[hc], t_xln]
                        if hc < 2 and t_tile_guard is not None:
                            wgu.append(t_tile_guard)
                        if hc >= 2:
                            wgu.append((AC, p.counts[AC] - 1))
                        for dc in range(NCH):
                            p.emit("tensor", lambda e, dc=dc, pg=pg, par=par: e.matmul(
                                pg[:], wg_b[par][:, dc * 128:(dc + 1) * 128],
                                xln[:, dc * TT:(dc + 1) * TT],
                                start=(dc == 0), stop=(dc == NCH - 1)),
                                waits=wgu if dc == 0 else [],
                                inc=([(WFG, 1)] if dc == NCH - 1 else None))
                        for dc in range(NCH):
                            p.emit("tensor", lambda e, dc=dc, pu=pu, par=par: e.matmul(
                                pu[:], wu_b[par][:, dc * 128:(dc + 1) * 128],
                                xln[:, dc * TT:(dc + 1) * TT],
                                start=(dc == 0), stop=(dc == NCH - 1)),
                                waits=[up_toks[hc]] if dc == 0 else [],
                                inc=([(WFU, 1)] if dc == NCH - 1 else None))
                        t_gu_mm = p.tok(WFU)
                        if hc + 2 < NHC:
                            load_gu(hc + 2)
                        sg, ue = sg_s[par], ue_s[par]
                        wsil = [t_gu_mm]
                        if hc >= 2:
                            wsil.append((DV, p.counts[DV] - 1))
                        p.emit("scalar", lambda e, sg=sg, pg=pg: e.activation(
                            sg[:], pg[:], AF.Silu, bias=0.0, scale=1.0), waits=wsil)
                        p.emit("scalar", lambda e, ue=ue, pu=pu: e.copy(ue[:], pu[:]),
                               inc=AC)
                        t_sil = p.tok(AC)
                        hdst = hbuf[:, hc * TT:(hc + 1) * TT]
                        hsq = sq_s[par]
                        whm = [t_sil]
                        if hc >= 2:
                            whm.append((RMS, p.counts[RMS] - 1))
                        p.emit("vector", lambda e, sg=sg, ue=ue, hdst=hdst: e.tensor_mul(
                            hdst, sg[:], ue[:]), waits=whm)
                        p.emit("vector", lambda e, hdst=hdst, hsq=hsq: e.tensor_mul(
                            hsq[:], hdst, hdst), inc=DV)
                        t_hsq = p.tok(DV)
                        p.emit("tensor", lambda e, hc=hc, hsq=hsq: e.matmul(
                            ps_A[0:1, :], onesb[:], hsq[:],
                            start=(hc == 0), stop=(hc == NHC - 1)),
                            waits=[t_hsq], inc=RMS)
                    t_rms = p.tok(RMS)

                    v = "vector"
                    p.efence(v, lambda e: e.tensor_copy(rows[0:1, 0:TT], ps_A[0:1, :]),
                             waits=[t_rms])
                    p.emit(v, lambda e: e.tensor_scalar(rows[0:1, TT:2 * TT], rows[0:1, 0:TT],
                                                        1.0 / HID, EPS_F32, ALU.mult, ALU.add),
                           inc=DV)
                    t_ms = p.tok(DV)
                    p.emit("scalar", lambda e: e.activation(rows[0:1, 2 * TT:3 * TT],
                                                            rows[0:1, TT:2 * TT], AF.Ln,
                                                            bias=0.0, scale=1.0),
                           waits=[t_ms], inc=AC)
                    t_l = p.tok(AC)
                    p.emit(v, lambda e: e.tensor_scalar_mul(rows[0:1, 2 * TT:3 * TT],
                                                            rows[0:1, 2 * TT:3 * TT], -0.5),
                           waits=[t_l], inc=DV)
                    t_nl = p.tok(DV)
                    p.emit("scalar", lambda e: e.activation(rows[0:1, 3 * TT:4 * TT],
                                                            rows[0:1, 2 * TT:3 * TT], AF.Exp,
                                                            bias=0.0, scale=1.0),
                           waits=[t_nl], inc=AC)
                    t_inv = p.tok(AC)
                    p.emit(v, lambda e: e.tensor_scalar_mul(rows[0:1, 4 * TT:5 * TT],
                                                            rows[0:1, 3 * TT:4 * TT], 0.5),
                           waits=[t_inv], inc=DV)
                    t_bi = p.tok(DV)
                    p.emit("tensor", lambda e: e.matmul(ps_A[:, :], onesf[:],
                                                        rows[0:1, 4 * TT:5 * TT],
                                                        start=True, stop=True),
                           waits=[t_bi, t_o1], inc=PE)
                    t_biv = p.tok(PE)
                    p.emit("scalar", lambda e: e.copy(binv[:], ps_A[:, :]),
                           waits=[t_biv], inc=AC)
                    t_binv = p.tok(AC)

                    t_prev_evac = None
                    for gi, (d0, nd) in enumerate(DDG):
                        ncol = nd * 128
                        wp_toks = {}

                        def load_wp(gb, d0=d0, ncol=ncol):
                            wv_ = [T_AG_P, (WFP, p.counts[WFP])]
                            if gb < 2 and t_tile_guard is not None:
                                wv_.append(t_tile_guard)
                            wp_toks[gb] = p.dma(
                                "sync",
                                wp_b[gb % 2][:].rearrange("p (r c) -> p r c", c=768)[:, :, 0:ncol],
                                ag_p.rearrange("(r p) c -> p r c", p=128)[
                                    :, 2 * gb:2 * gb + 2, d0 * 128:d0 * 128 + ncol],
                                waits=wv_, sem=WLD)

                        load_wp(0)
                        load_wp(1)
                        for hc in range(NHC):
                            gb, sub = hc // 2, hc % 2
                            par = gb % 2
                            wpr = [wp_toks[gb]]
                            if hc == 0:
                                wpr.append((AC, p.counts[AC]))
                                if t_prev_evac is not None:
                                    wpr.append(t_prev_evac)
                            for di in range(nd):
                                last = (di == nd - 1 and sub == 1)
                                inc_ = [(WFP, 1)] if last else None
                                p.emit("tensor", lambda e, di=di, hc=hc, par=par, sub=sub: e.matmul(
                                    ps_bank[di][:],
                                    wp_b[par][:, sub * 768 + di * 128: sub * 768 + (di + 1) * 128],
                                    hbuf[:, hc * TT:(hc + 1) * TT],
                                    start=(hc == 0), stop=(hc == NHC - 1)),
                                    waits=wpr if di == 0 else [],
                                    inc=inc_)
                            if sub == 1 and gb + 2 < NHC // 2:
                                load_wp(gb + 2)
                        t_proj = p.tok(WFP)
                        for di in range(nd):
                            dd = d0 + di
                            xsl = xT[:, dd * TLOC + tt0: dd * TLOC + tt0 + TT]
                            p.emit("vector", lambda e, di=di, xsl=xsl: e.tensor_mul(
                                rtmp[:], ps_bank[di][:], binv[:]),
                                waits=[t_proj, t_binv])
                            p.emit("vector", lambda e, xsl=xsl: e.tensor_add(xsl, xsl, rtmp[:]),
                                   inc=(DV if di == nd - 1 else None))
                        t_prev_evac = p.tok(DV)
                    t_tile_guard = t_prev_evac
                return p.tok(DV)

        t_after = None
        if "ffn1" in parts:
            t_after = ffn_call("f1", t_quant_ac)

        if "attn" in parts:
            xnT = av_(0, NCH * TLOC)
            xl2 = av_(16384, NCH * TT)
            vTf = av_(24576, KVH * TLOC)
            aoT = av_(16384, NCH * TLOC)
            with contextlib.ExitStack() as actx:
                asb = lambda nm, shp, dt: actx.enter_context(nc.sbuf_tensor(nm, shp, dt))
                aps = lambda nm, shp, dt: actx.enter_context(nc.psum_tensor(nm, shp, dt))
                kT = asb("kT", [128, KVH * 1536], BF16)
                vA = asb("vA", [128, 12 * 512], BF16)
                qraw = asb("qraw", [128, TT], F32)
                ropeC2 = asb("ropeC2", [128, 1024], BF16)
                ropeS2 = asb("ropeS2", [128, 1024], BF16)
                pswap = asb("pswap_t", [128, 128], F32)
                wqa = [asb(f"wqa{i}", [128, NCH * 128], BF16) for i in range(2)]
                ngmx = asb("ngmx", [128, 1], F32)
                rsum = asb("rsum", [128, 1], F32)
                rrcp = asb("rrcp", [128, 1], F32)
                masks = scr16[:, 0:NQT * KWIN]
                qrot1 = scr16[:, 5120:6144]
                probs = scr16[:, 6144:6784]
                pmm = scr16[:, 6784:7424]
                pT1 = scr16[:, 7424:8064]

                t_mask = p.dma("sync", masks.rearrange("p (q k) -> p q k", k=KWIN),
                               mask_in.rearrange("q p k -> p q k"),
                               waits=[t_after])
                t_ropeC = p.dma("sync", ropeC2[:], ropeC_in[:])
                t_ropeS = p.dma("sync", ropeS2[:], ropeS_in[:])
                t_psw = p.dma("sync", pswap[:], pswap_in[:])
                ps_A2 = aps("psA2", [128, TT], F32)
                ps_B2 = aps("psB2", [128, TT], F32)
                ps_q = aps("psq", [128, TT], F32)
                ps_sc = aps("ps_sc", [128, KWIN], F32)
                ps_pT = aps("ps_pT", [128, KWIN], BF16)
                ps_av = aps("ps_av", [128, 128], F32)

                pre = [t_after]
                for tt in range(NTT):
                    tt0 = tt * TT
                    t_bcs = ln_block(ps_A2, ps_B2, tt0, pre)
                    pre = []
                    for c in range(NCH):
                        dst = xl2[:, c * TT:(c + 1) * TT]
                        xsl = xT[:, c * TLOC + tt0: c * TLOC + tt0 + TT]
                        p.emit("vector", lambda e, dst=dst, xsl=xsl: e.tensor_mul(dst, xsl, bcA[:]),
                               waits=[t_bcs])
                        p.emit("vector", lambda e, dst=dst: e.tensor_add(dst, dst, bcB[:]),
                               inc=(DV if c == NCH - 1 else None))
                    t_xl2 = p.tok(DV)
                    rms_toks = []
                    for c in range(NCH):
                        par = c % 2
                        sq = sq_s[par]
                        w0 = [t_xl2]
                        if c >= 2:
                            w0.append(rms_toks[c - 2])
                        p.emit("vector", lambda e, c=c, sq=sq: e.tensor_mul(
                            sq[:], xl2[:, c * TT:(c + 1) * TT], xl2[:, c * TT:(c + 1) * TT]),
                            waits=w0, inc=DV)
                        tdv = p.tok(DV)
                        p.emit("tensor", lambda e, c=c, sq=sq: e.matmul(
                            ps_A2[0:1, :], onesb[:], sq[:], start=(c == 0), stop=(c == NCH - 1)),
                            waits=[tdv], inc=PE)
                        rms_toks.append(p.tok(PE))
                    t_st2 = rms_toks[-1]
                    v = "vector"
                    p.efence(v, lambda e: e.tensor_copy(rows[0:1, 0:TT], ps_A2[0:1, :]),
                             waits=[t_st2])
                    p.emit(v, lambda e: e.tensor_scalar(rows[0:1, TT:2 * TT], rows[0:1, 0:TT],
                                                        1.0 / D, EPS_ATTN, ALU.mult, ALU.add),
                           inc=DV)
                    t_ms = p.tok(DV)
                    p.emit("scalar", lambda e: e.activation(rows[0:1, 2 * TT:3 * TT],
                                                            rows[0:1, TT:2 * TT], AF.Ln,
                                                            bias=0.0, scale=1.0),
                           waits=[t_ms], inc=AC)
                    t_l = p.tok(AC)
                    p.emit(v, lambda e: e.tensor_scalar_mul(rows[0:1, 2 * TT:3 * TT],
                                                            rows[0:1, 2 * TT:3 * TT], -0.5),
                           waits=[t_l], inc=DV)
                    t_nl = p.tok(DV)
                    p.emit("scalar", lambda e: e.activation(rows[0:1, 3 * TT:4 * TT],
                                                            rows[0:1, 2 * TT:3 * TT], AF.Exp,
                                                            bias=0.0, scale=1.0),
                           waits=[t_nl], inc=AC)
                    t_r2 = p.tok(AC)
                    p.emit("tensor", lambda e: e.matmul(ps_A2[:, :], onesf[:],
                                                        rows[0:1, 3 * TT:4 * TT],
                                                        start=True, stop=True),
                           waits=[t_r2, t_o1], inc=PE)
                    t_b2 = p.tok(PE)
                    p.emit("scalar", lambda e: e.copy(bcA[:], ps_A2[:, :]), waits=[t_b2],
                           inc=AC)
                    t_bc2 = p.tok(AC)
                    for c in range(NCH):
                        dst = xnT[:, c * TLOC + tt0: c * TLOC + tt0 + TT]
                        p.emit(v, lambda e, c=c, dst=dst: e.tensor_mul(
                            dst, xl2[:, c * TT:(c + 1) * TT], bcA[:]), waits=[t_bc2])
                        p.emit(v, lambda e, c=c, dst=dst: e.tensor_scalar_mul(
                            dst, dst, lnw[:, c:c + 1]),
                            waits=[t_lnw], inc=(DV if c == NCH - 1 else None))
                t_xn = p.tok(DV)

                nuse = [0]

                def wqa_load(src_ap, waits_extra=()):
                    par = nuse[0] % 2
                    w = list(waits_extra)
                    if nuse[0] < 2:
                        w.append(t_xn)
                    if nuse[0] >= 2:
                        w.append((WFA, p.counts[WFA] - 1))
                    tok = p.dma("gpsimd", wqa[par][:].rearrange("p (c j) -> p c j", j=128),
                                src_ap, waits=w)
                    nuse[0] += 1
                    return par, tok

                t_qevac_prev = None
                t_rope_prev = None
                kvhs = range(KVH) if "nokv" not in parts else range(0)
                for kvh in kvhs:
                    par, t_wk = wqa_load(
                        wk_in.rearrange("(c p) j -> p c j", p=128)[:, :, kvh * 128:(kvh + 1) * 128])
                    base = kvh * 1536 + 512
                    for th in range(2):
                        for dc in range(NCH):
                            last = dc == NCH - 1
                            p.emit("tensor", lambda e, dc=dc, th=th, par=par: e.matmul(
                                ps_q[:], wqa[par][:, dc * 128:(dc + 1) * 128],
                                xnT[:, dc * TLOC + th * TT: dc * TLOC + (th + 1) * TT],
                                start=(dc == 0), stop=(dc == NCH - 1)),
                                waits=([t_wk, t_xn] +
                                       ([(AC, p.counts[AC])] if (th > 0 or kvh > 0) else []))
                                if dc == 0 else [],
                                inc=([(WFA, 1)] if (last and th == 1) else
                                     ([(PE, 1)] if last else None)))
                        t_kmm = p.tok(PE) if th == 0 else p.tok(WFA)
                        p.emit("scalar", lambda e: e.copy(qraw[:], ps_q[:]),
                               waits=[t_kmm if th == 0 else (WFA, p.counts[WFA]),
                                      (DV, p.counts[DV])], inc=AC)
                        t_ke = p.tok(AC)
                        o = th * TT
                        dstk = kT[:, base + o:base + o + TT]
                        if "norope" in parts:
                            p.emit(v, lambda e, dstk=dstk: e.tensor_copy(dstk, qraw[:]),
                                   waits=[t_ke], inc=DV)
                        else:
                            p.emit("tensor", lambda e: e.matmul(
                                ps_sc[:, 0:TT], pswap[:], qraw[:], start=True, stop=True),
                                waits=[t_ke, t_psw, (DV, p.counts[DV])], inc=PE)
                            t_sw = p.tok(PE)
                            p.emit(v, lambda e, o=o: e.scalar_tensor_tensor(
                                rtmp[:], qraw[:], 1.0, ropeC2[:, o:o + TT], ALU.mult, ALU.mult),
                                waits=[t_ropeC, t_ropeS, t_ke])
                            p.emit(v, lambda e, dstk=dstk, o=o: e.scalar_tensor_tensor(
                                dstk, ps_sc[:, 0:TT], 1.0, ropeS2[:, o:o + TT], ALU.mult, ALU.mult),
                                waits=[t_sw])
                            p.emit(v, lambda e, dstk=dstk: e.tensor_add(dstk, dstk, rtmp[:]),
                                   inc=DV)
                t_krope = p.tok(DV)
                for kvh in (kvhs if "nov" not in parts else range(0)):
                    par, t_wv = wqa_load(
                        wv_in.rearrange("(c p) j -> p c j", p=128)[:, :, kvh * 128:(kvh + 1) * 128],
                        waits_extra=[(DV, p.counts[DV])])
                    for th in range(2):
                        for dc in range(NCH):
                            last = dc == NCH - 1
                            p.emit("tensor", lambda e, dc=dc, th=th, par=par: e.matmul(
                                ps_q[:], wqa[par][:, dc * 128:(dc + 1) * 128],
                                xnT[:, dc * TLOC + th * TT: dc * TLOC + (th + 1) * TT],
                                start=(dc == 0), stop=(dc == NCH - 1)),
                                waits=[t_wv, (AC, p.counts[AC])] if dc == 0 else [],
                                inc=([(WFA, 1)] if (last and th == 1) else
                                     ([(PE, 1)] if last else None)))
                        t_vmm = p.tok(PE) if th == 0 else p.tok(WFA)
                        dstv = vTf[:, kvh * TLOC + th * TT: kvh * TLOC + (th + 1) * TT]
                        p.emit("scalar", lambda e, dstv=dstv, th=th: e.copy(dstv, ps_q[:]),
                               waits=[t_vmm if th == 0 else (WFA, p.counts[WFA])], inc=AC)
                t_vf = p.tok(AC)
                tcs = range(8) if ("nokv" not in parts and "nov" not in parts and "novt" not in parts) else range(0)
                t_vev = None
                for tc in tcs:
                    for kvh in range(KVH):
                        wtt = [t_vf, t_id]
                        if t_vev is not None:
                            wtt.append(t_vev)
                        p.emit("tensor", lambda e, tc=tc, kvh=kvh: e.transpose(
                            ps_pT[:, kvh * 128:(kvh + 1) * 128],
                            vTf[:, kvh * TLOC + tc * 128: kvh * TLOC + (tc + 1) * 128],
                            ident[:]), waits=wtt if kvh == 0 else [],
                            inc=(PE if kvh == KVH - 1 else None))
                    t_tpv = p.tok(PE)
                    p.emit("scalar", lambda e, tc=tc: e.copy(
                        vA[:, (4 + tc) * 512: (5 + tc) * 512], ps_pT[:, 0:512]),
                        waits=[t_tpv], inc=AC)
                    t_vev = p.tok(AC)
                t_vA = p.tok(AC)

                do_xch = "noxch" not in parts
                if do_xch:
                    p.dma("gpsimd", kv_in[0:512, :].rearrange("(c p) t -> p c t", p=128),
                          kT[:].rearrange("p (c t) -> p c t", t=1536)[:, :, 1024:1536],
                          waits=[t_krope])
                    p.dma("gpsimd", kv_in[512:1024, :].rearrange("(c p) t -> p c t", p=128),
                          vA[:].rearrange("p (c t) -> p c t", t=512)[:, 8:12, :],
                          waits=[t_vA])
                    t_kvw = p.tok(GLD)
                    p.emit("gpsimd", lambda e: e.collective_compute(
                        "AllGather", ALU.bypass, replica_groups=[list(range(NC))],
                        ins=[kv_in[:]], outs=[kv_out[:]]), waits=[t_kvw], inc=CC)
                    t_agkv = p.tok(CC)

                def rb_k(e):
                    if "noro" in parts:
                        src = kv_out[0:512, :]
                    else:
                        from concourse.expressions import RuntimeValue
                        with e.register("offk") as r:
                            e.reg_load(r, offk_sb[0:1, 0:1])
                            src = kv_out[bass.ds(RuntimeValue(r), 512), :]
                    return e.dma_start(
                        out=kT[:].rearrange("p (c t) -> p c t", t=1536)[:, :, 0:512],
                        in_=src.rearrange("(c p) t -> p c t", p=128))
                if do_xch:
                    p.emit("gpsimd", rb_k, waits=[t_agkv, t_offk], inc=[(GLD, 16)])

                def rb_v(e):
                    if "noro" in parts:
                        src = kv_out[512:1024, :]
                    else:
                        from concourse.expressions import RuntimeValue
                        with e.register("offv") as r:
                            e.reg_load(r, offv_sb[0:1, 0:1])
                            src = kv_out[bass.ds(RuntimeValue(r), 512), :]
                    return e.dma_start(
                        out=vA[:].rearrange("p (c t) -> p c t", t=512)[:, 0:4, :],
                        in_=src.rearrange("(c p) t -> p c t", p=128))
                if do_xch:
                    p.emit("gpsimd", rb_v, waits=[t_offv], inc=[(GLD, 16)])
                t_halo = p.tok(GLD)

                if dbg:
                    p.dma("sync", dbg_xn[:], xnT, waits=[t_xn], sem=ST)
                    p.dma("sync", dbg_kT[:], kT[:], waits=[t_halo, t_krope], sem=ST)
                t_ao_toks = []
                t_qrope_prev = None
                heads = range(H) if "nohead" not in parts else range(0)
                for h in heads:
                    kvh = h // 4
                    par, t_wq = wqa_load(
                        wq_in.rearrange("(c p) j -> p c j", p=128)[:, :, h * 128:(h + 1) * 128])
                    qv = qrot1
                    for th in range(2):
                        for dc in range(NCH):
                            last = dc == NCH - 1
                            p.emit("tensor", lambda e, dc=dc, th=th, par=par: e.matmul(
                                ps_q[:], wqa[par][:, dc * 128:(dc + 1) * 128],
                                xnT[:, dc * TLOC + th * TT: dc * TLOC + (th + 1) * TT],
                                start=(dc == 0), stop=(dc == NCH - 1)),
                                waits=[t_wq, t_xn, (AC, p.counts[AC])] if dc == 0 else [],
                                inc=([(WFA, 1)] if (last and th == 1) else
                                     ([(PE, 1)] if last else None)))
                        t_qmm = p.tok(PE) if th == 0 else p.tok(WFA)
                        p.emit("scalar", lambda e: e.copy(qraw[:], ps_q[:]),
                               waits=[t_qmm if th == 0 else (WFA, p.counts[WFA]),
                                      (DV, p.counts[DV])], inc=AC)
                        t_qe = p.tok(AC)
                        o = th * TT
                        p.emit("tensor", lambda e: e.matmul(
                            ps_sc[:, 0:TT], pswap[:], qraw[:], start=True, stop=True),
                            waits=[t_qe, t_psw, (DV, p.counts[DV]),
                                   (AC, p.counts[AC])], inc=PE)
                        t_sw = p.tok(PE)
                        dstq = qv[:, o:o + TT]
                        p.emit(v, lambda e, o=o: e.scalar_tensor_tensor(
                            rtmp[:], qraw[:], SCALE, ropeC2[:, o:o + TT], ALU.mult, ALU.mult),
                            waits=[t_ropeC, t_ropeS, t_qe])
                        p.emit(v, lambda e, dstq=dstq, o=o: e.scalar_tensor_tensor(
                            dstq, ps_sc[:, 0:TT], SCALE, ropeS2[:, o:o + TT], ALU.mult, ALU.mult),
                            waits=[t_sw])
                        p.emit(v, lambda e, dstq=dstq: e.tensor_add(dstq, dstq, rtmp[:]),
                               inc=DV)
                    t_qrope_prev = p.tok(DV)
                    t_qrope = t_qrope_prev

                    t_exp_prev = None
                    t_ave_prev = None
                    t_pTe_prev = None
                    t_tp_last = None
                    for qt in range(NQT):
                        kbase = kvh * 1536 + qt * 128
                        wqk = [t_qrope, t_halo]
                        if t_exp_prev is not None:
                            wqk.append(t_exp_prev)
                        p.emit("tensor", lambda e, kbase=kbase, qv=qv, qt=qt: e.matmul(
                            ps_sc[:, 0:512], qv[:, qt * 128:(qt + 1) * 128],
                            kT[:, kbase:kbase + 512], start=True, stop=True), waits=wqk)
                        p.emit("tensor", lambda e, kbase=kbase, qv=qv, qt=qt: e.matmul(
                            ps_sc[:, 512:640], qv[:, qt * 128:(qt + 1) * 128],
                            kT[:, kbase + 512:kbase + 640], start=True, stop=True), inc=PE)
                        t_sc = p.tok(PE)
                        p.emit(v, lambda e: e.reduce_max(ngmx[:], ps_sc[:], axis=AX.X,
                                                         negate=True), waits=[t_sc], inc=DV)
                        t_ng = p.tok(DV)
                        p.emit("scalar", lambda e: e.activation(
                            probs[:], ps_sc[:], AF.Exp, bias=ngmx[:, 0:1], scale=1.0),
                            waits=[t_ng], inc=AC)
                        t_exp_prev = p.tok(AC)
                        p.emit(v, lambda e, qt=qt: e.scalar_tensor_tensor(
                            pmm[:], probs[:], 1.0, masks[:, qt * KWIN:(qt + 1) * KWIN],
                            ALU.mult, ALU.mult, accum_out=rsum[:]),
                            waits=[t_exp_prev, t_mask, t_tp_last], inc=FEN)
                        p.emit(v, lambda e: None, waits=[p.tok(FEN)])
                        p.efence(v, lambda e: e.reciprocal(rrcp[:], rsum[:]))
                        p.emit(v, lambda e: e.tensor_scalar_mul(pmm[:], pmm[:], rrcp[:, 0:1]),
                               inc=DV)
                        t_pn = p.tok(DV)
                        wtp = [t_pn]
                        if t_pTe_prev is not None:
                            wtp.append(t_pTe_prev)
                        for c in range(5):
                            p.emit("tensor", lambda e, c=c: e.transpose(
                                ps_pT[:, c * 128:(c + 1) * 128], pmm[:, c * 128:(c + 1) * 128],
                                ident[:]), waits=wtp if c == 0 else [],
                                inc=(PE if c == 4 else None))
                        t_tp = p.tok(PE)
                        t_tp_last = t_tp
                        pT = pT1
                        p.emit("scalar", lambda e, pT=pT: e.copy(pT[:], ps_pT[:]),
                               waits=[t_tp], inc=AC)
                        t_pTe_prev = p.tok(AC)
                        wav = [t_pTe_prev, t_vA]
                        if t_ave_prev is not None:
                            wav.append(t_ave_prev)
                        for c in range(5):
                            vseg = (qt + c) * 512 + kvh * 128
                            p.emit("tensor", lambda e, c=c, vseg=vseg, pT=pT: e.matmul(
                                ps_av[:], vA[:, vseg:vseg + 128], pT[:, c * 128:(c + 1) * 128],
                                start=(c == 0), stop=(c == 4)),
                                waits=wav if c == 0 else [],
                                inc=(PE if c == 4 else None))
                        t_avm = p.tok(PE)
                        p.emit("scalar", lambda e, h=h, qt=qt: e.copy(
                            aoT[:, h * TLOC + qt * 128: h * TLOC + (qt + 1) * 128],
                            ps_av[:]), waits=[t_avm], inc=AC)
                        t_ave_prev = p.tok(AC)
                t_ao = p.tok(AC)
                if dbg:
                    p.dma("sync", dbg_qv[:], qrot1, waits=[t_qrope_prev], sem=ST)
                    p.dma("sync", dbg_ao[:], aoT, waits=[t_ao], sem=ST)

                dds = range(NCH) if "nowo" not in parts else range(0)
                for dd in dds:
                    par, t_wo = wqa_load(
                        wo_in.rearrange("(c p) j -> p c j", p=128)[:, :, dd * 128:(dd + 1) * 128])
                    for th in range(2):
                        for oc in range(NCH):
                            last = oc == NCH - 1
                            p.emit("tensor", lambda e, oc=oc, th=th, par=par: e.matmul(
                                ps_q[:], wqa[par][:, oc * 128:(oc + 1) * 128],
                                aoT[:, oc * TLOC + th * TT: oc * TLOC + (th + 1) * TT],
                                start=(oc == 0), stop=(oc == NCH - 1)),
                                waits=[t_wo, t_ao, (DV, p.counts[DV])] if oc == 0 else [],
                                inc=([(WFA, 1)] if (last and th == 1) else
                                     ([(PE, 1)] if last else None)))
                        t_womm = p.tok(PE) if th == 0 else p.tok(WFA)
                        xsl = xT[:, dd * TLOC + th * TT: dd * TLOC + (th + 1) * TT]
                        p.emit("vector", lambda e, xsl=xsl: e.tensor_add(xsl, xsl, ps_q[:]),
                               waits=[t_womm if th == 0 else (WFA, p.counts[WFA])],
                               inc=DV)
                t_after = p.tok(DV)
                if "nowo" in parts:
                    t_after = t_xn
                p.wait("tensor", t_after)

        if "ffn2" in parts:
            t_after = ffn_call("f2", t_after)

        if dbg:
            p.dma("sync", dbg_gu[:], ag_gu[0:512, :], waits=[T_AG_GU], sem=ST)
            p.dma("sync", dbg_p[:], ag_p[0:256, :], waits=[T_AG_P], sem=ST)
            p.dma("sync", dbg_xln[:], scr16[:, 0:NCH * TT], waits=[p.tok(DV)], sem=ST)
            p.dma("sync", dbg_h[:], arena[:, 0:4 * TT], waits=[p.tok(DV)], sem=ST)
        t_done = [p.tok(DV), p.tok(AC)]
        p.emit("sync", lambda e: e.dma_start(
            out=out_d.rearrange("(c p) t -> p c t", p=128),
            in_=xT[:].rearrange("p (c t) -> p c t", t=TLOC)),
            waits=t_done, inc=[(ST, 16)])
        p.emit("sync", lambda e: None, waits=[p.tok(ST)])

        block = ctx.enter_context(nc.Block())
        p.replay(block)
    return nc


def _perm_cols(n_heads, hd=128):
    idx = []
    for hb in range(n_heads):
        base = hb * hd
        idx.extend([base + 2 * i for i in range(hd // 2)])
        idx.extend([base + 2 * i + 1 for i in range(hd // 2)])
    return np.array(idx)


def _host_inputs(L0_attn_x, L0_attn_ln_w, L0_attn_wq, L0_attn_wk, L0_attn_wv,
                 L0_attn_wo, L0_ffn_w_gate_up, L0_ffn_w_proj):
    x = np.asarray(L0_attn_x, dtype=np.float32)
    lnw_v = np.asarray(L0_attn_ln_w, dtype=np.float32)
    wq = np.ascontiguousarray(np.asarray(L0_attn_wq, dtype=np.float32))
    wk = np.ascontiguousarray(np.asarray(L0_attn_wk, dtype=np.float32))
    wv = np.ascontiguousarray(np.asarray(L0_attn_wv, dtype=np.float32))
    wo = np.ascontiguousarray(np.asarray(L0_attn_wo, dtype=np.float32))
    wgu = np.asarray(L0_ffn_w_gate_up, dtype=np.float32)
    wp = np.asarray(L0_ffn_w_proj, dtype=np.float32)

    ident = np.eye(128, dtype=bf)
    onesf = np.ones((1, 128), dtype=np.float32)
    onesb = np.ones((128, 1), dtype=bf)
    lnw_t = np.ascontiguousarray(lnw_v.reshape(NCH, 128).T)
    inv_freq = (1.0 / 10000.0 ** (np.arange(0, HD, 2, dtype=np.float32) / HD))
    pswap_m = np.zeros((128, 128), dtype=np.float32)
    for i in range(0, 128, 2):
        pswap_m[i, i + 1] = 1.0
        pswap_m[i + 1, i] = 1.0

    in_maps = []
    for r in range(NC):
        b, half = r // 2, r % 2
        x0T = np.ascontiguousarray(x[b, half * TLOC:(half + 1) * TLOC, :].T)
        pos = np.arange(half * TLOC, half * TLOC + TLOC, dtype=np.float32)
        ang = pos[None, :] * inv_freq[:, None]
        cosv = np.cos(ang).astype(np.float32)
        sinv = np.sin(ang).astype(np.float32)
        ropeC = np.empty((128, TLOC), dtype=np.float32)
        ropeC[0::2] = cosv
        ropeC[1::2] = cosv
        ropeS = np.empty((128, TLOC), dtype=np.float32)
        ropeS[0::2] = -sinv
        ropeS[1::2] = sinv
        ropeC = np.ascontiguousarray(ropeC.astype(bf))
        ropeS = np.ascontiguousarray(ropeS.astype(bf))
        qts = np.arange(NQT)
        i_ = np.arange(128)
        col = np.arange(KWIN)
        qpos = half * TLOC + qts[:, None, None] * 128 + i_[None, :, None]
        kpos = half * TLOC + qts[:, None, None] * 128 + col[None, None, :] - 512
        valid = (kpos >= qpos - 512) & (kpos <= qpos) & (kpos >= 0)
        maskmul = np.ascontiguousarray(valid.astype(bf))
        partner = r ^ 1
        in_maps.append({
            "x0T": x0T,
            "wq": wq, "wk": wk, "wv": wv, "wo": wo,
            "gu_raw": np.ascontiguousarray(wgu[r * (HID2 // NC):(r + 1) * (HID2 // NC), :]),
            "wp_raw": np.ascontiguousarray(wp[:, r * (HID // NC):(r + 1) * (HID // NC)]),
            "lnw": lnw_t,
            "ident": ident, "onesf": onesf, "onesb": onesb,
            "ropeC": ropeC, "ropeS": ropeS, "pswap": pswap_m,
            "maskmul": maskmul,
            "off_k": np.array([[partner * 1024]], dtype=np.int32),
            "off_v": np.array([[partner * 1024 + 512]], dtype=np.int32),
        })
    return in_maps


def kernel(L0_attn_x, L0_attn_ln_w, L0_attn_wq, L0_attn_wk, L0_attn_wv,
           L0_attn_wo, L0_ffn_w_gate_up, L0_ffn_w_proj):
    in_maps = _host_inputs(L0_attn_x, L0_attn_ln_w, L0_attn_wq, L0_attn_wk,
                           L0_attn_wv, L0_attn_wo, L0_ffn_w_gate_up, L0_ffn_w_proj)
    nc = build_program()
    res = run_bass_kernel_spmd(nc, in_maps, core_ids=list(range(NC)))
    out = np.empty((B, S, D), dtype=np.float32)
    for r in range(NC):
        b, half = r // 2, r % 2
        out[b, half * TLOC:(half + 1) * TLOC, :] = res.results[r]["out"].T
    return out


# version 04
# speedup vs baseline: 1.0378x; 1.0378x over previous
import contextlib
import numpy as np
import ml_dtypes

import concourse.bass as bass
import concourse.mybir as mybir
from concourse.bass_utils import run_bass_kernel_spmd

F32 = mybir.dt.float32
BF16 = mybir.dt.bfloat16
I32 = mybir.dt.int32
ALU = mybir.AluOpType
AF = mybir.ActivationFunctionType
AX = mybir.AxisListType
bf = ml_dtypes.bfloat16

B, S, D = 4, 2048, 2048
NC = 8
TLOC = 1024
NCH = D // 128
TT = 512
NTT = TLOC // TT
H, KVH, HD, W = 16, 4, 128, 512
HID2, HID = 16384, 8192
NHC = HID // 128
SCALE = float(1.0 / np.sqrt(HD))
EPS_LN = 1e-5
EPS_ATTN = 1e-6
EPS_F32 = float(np.finfo(np.float32).eps)
NQT = TLOC // 128
KWIN = 640
DDG = [(0, 6), (6, 6), (12, 4)]

ENGINES = ["sync", "gpsimd", "vector", "scalar", "tensor"]


class Prog:

    def __init__(self, nc, ctx):
        self.nc = nc
        self.ctx = ctx
        self.q = {e: [] for e in ENGINES}
        self.sems = {}
        self.counts = {}
        self.waited = {e: {} for e in ENGINES}

    def sem(self, name):
        self.sems[name] = self.ctx.enter_context(self.nc.semaphore(name))
        self.counts[name] = 0
        return name

    def emit(self, eng, fn, waits=(), inc=None):
        waits = [t for t in waits if t is not None]
        waits = [(s, v) for (s, v) in waits if v > 0 and self.waited[eng].get(s, 0) < v]
        for s, v in waits:
            assert v <= self.counts[s], f"wait on future token {s}={v} (now {self.counts[s]})"
            self.waited[eng][s] = v
        if inc is None:
            incs = []
        elif isinstance(inc, str):
            incs = [(inc, 1)]
        else:
            incs = list(inc)
        for s, by in incs:
            self.counts[s] += by
        sems = self.sems
        carried = eng in ("vector", "scalar") and any(s != "fen" for s, _ in incs)
        prog = self

        def run(engine, fn=fn, waits=waits, incs=incs, carried=carried, eng=eng):
            for s, v in waits:
                engine.wait_ge(sems[s], v)
            ins = fn(engine)
            if carried:
                car = engine.drain(fusable=False)
                for s, by in incs:
                    car = car.then_inc(sems[s], by)
            else:
                for s, by in incs:
                    ins = ins.then_inc(sems[s], by)

        self.q[eng].append(run)

    def tok(self, name):
        return (name, self.counts[name])

    def wait(self, eng, *toks):
        self.emit(eng, lambda e: None, waits=list(toks))

    def efence(self, eng, fn, waits=()):
        self.emit(eng, fn, waits=waits, inc=self.FEN)
        self.emit(eng, lambda e: None, waits=[self.tok(self.FEN)])

    def dma(self, eng, out, in_, waits=(), sem=None):
        sem = sem or (self.GLD if eng == "gpsimd" else self.WLD)
        self.emit(eng, lambda e, o=out, i=in_: e.dma_start(out=o, in_=i),
                  waits=waits, inc=[(sem, 16)])
        return self.tok(sem)

    def replay(self, block):
        for name in ENGINES:
            fns = self.q[name]
            getattr(block, name)(lambda engine, fns=fns: [f(engine) for f in fns])


def build_program(parts=("ffn1", "attn", "ffn2")):
    nc = bass.Bass("TRN2", target_bir_lowering=False, debug=False)

    def din(name, shp, dt=F32):
        return nc.dram_tensor(name, shp, dt, kind="ExternalInput").ap()

    x0T = din("x0T", [D, TLOC])
    wq_in = din("wq", [D, D])
    wk_in = din("wk", [D, KVH * HD])
    wv_in = din("wv", [D, KVH * HD])
    wo_in = din("wo", [D, D])
    gu_raw = din("gu_raw", [HID2 // NC, D])
    wp_raw = din("wp_raw", [D, HID // NC])
    lnw_in = din("lnw", [128, NCH])
    ident_in = din("ident", [128, 128], BF16)
    ones_in = din("onesf", [1, 128])
    onesb_in = din("onesb", [128, 1], BF16)
    ropeC_in = din("ropeC", [128, 1024], BF16)
    ropeS_in = din("ropeS", [128, 1024], BF16)
    pswap_in = din("pswap", [128, 128])
    mask_in = din("maskmul", [NQT, 128, KWIN], BF16)
    off_k_in = din("off_k", [1, 1], I32)
    off_v_in = din("off_v", [1, 1], I32)

    out_d = nc.dram_tensor("out", [D, TLOC], F32, kind="ExternalOutput").ap()
    dbg = "dbg" in parts
    if dbg:
        dbg_f = nc.dram_tensor("dbg_f", [128, D], BF16, kind="ExternalOutput").ap()
        dbg_fwp = nc.dram_tensor("dbg_fwp", [128, 2 * 1024], BF16, kind="ExternalOutput").ap()
        dbg_st2 = nc.dram_tensor("dbg_st2", [128, D], BF16, kind="ExternalOutput").ap()
        dbg_xn = nc.dram_tensor("dbg_xn", [128, NCH * TLOC], BF16, kind="ExternalOutput").ap()
        dbg_kT = nc.dram_tensor("dbg_kT", [128, KVH * 1536], BF16, kind="ExternalOutput").ap()
        dbg_qv = nc.dram_tensor("dbg_qv", [128, TLOC], BF16, kind="ExternalOutput").ap()
        dbg_ao = nc.dram_tensor("dbg_ao", [128, NCH * TLOC], BF16, kind="ExternalOutput").ap()
        dbg_stage = nc.dram_tensor("dbg_stage", [128, D], BF16, kind="ExternalOutput").ap()
        dbg_scr = nc.dram_tensor("dbg_scr", [128, D], BF16, kind="ExternalOutput").ap()
        dbg_gu = nc.dram_tensor("dbg_gu", [512, D], BF16, kind="ExternalOutput").ap()
        dbg_p = nc.dram_tensor("dbg_p", [256, D], BF16, kind="ExternalOutput").ap()
        dbg_xln = nc.dram_tensor("dbg_xln", [128, NCH * TT], BF16, kind="ExternalOutput").ap()
        dbg_h = nc.dram_tensor("dbg_h", [128, 4 * TT], BF16, kind="ExternalOutput").ap()

    ag_gu_in = nc.dram_tensor("ag_gu_in", [HID2 // NC, D], BF16, kind="Internal").ap()
    ag_gu = nc.dram_tensor("ag_gu", [HID2, D], BF16, kind="Internal", addr_space="Shared").ap()
    ag_p_in = nc.dram_tensor("ag_p_in", [HID // NC, D], BF16, kind="Internal").ap()
    ag_p = nc.dram_tensor("ag_p", [HID, D], BF16, kind="Internal", addr_space="Shared").ap()
    kv_in = nc.dram_tensor("kv_in", [1024, 512], BF16, kind="Internal").ap()
    kv_out = nc.dram_tensor("kv_out", [NC * 1024, 512], BF16, kind="Internal", addr_space="Shared").ap()

    with contextlib.ExitStack() as ctx:
        sb = lambda nm, shp, dt: ctx.enter_context(nc.sbuf_tensor(nm, shp, dt))

        xT = sb("xT", [128, NCH * TLOC], F32)
        arena = sb("arena", [128, 32768], BF16)
        scr16 = sb("scr16", [128, 8192], BF16)
        rows = sb("rows", [1, 6 * TT], F32)
        bcA = sb("bcA", [128, TT], F32)
        bcB = sb("bcB", [128, TT], F32)
        rtmp = sb("rtmp", [128, TT], F32)
        lnw = sb("lnw_t", [128, NCH], F32)
        ident = sb("ident_t", [128, 128], BF16)
        onesf = sb("onesf_t", [1, 128], F32)
        onesb = sb("onesb_t", [128, 1], BF16)
        offk_sb = sb("offk_sb", [1, 1], I32)
        offv_sb = sb("offv_sb", [1, 1], I32)
        xb_s = [sb(f"xb{i}", [128, TT], BF16) for i in range(2)]
        sq_s = [sb(f"sq{i}", [128, TT], BF16) for i in range(2)]

        dumt = sb("dumt", [1, 16], F32)
        dumb = sb("dumb", [1, 16], BF16)
        p = Prog(nc, ctx)
        p.dum = dumt

        def mk_pe_mark(ps_mark):
            def pe_mark(waits=(), inc=None):
                p.emit("tensor", lambda e: e.matmul(ps_mark[0:1, 0:1], dumb[0:1, 0:1],
                                                    dumb[0:1, 8:9], start=True, stop=True),
                       waits=waits, inc=inc)
            return pe_mark
        WLD = p.sem("wld"); p.WLD = WLD
        GLD = p.sem("gld"); p.GLD = GLD
        FEN = p.sem("fen"); p.FEN = FEN
        DV = p.sem("dv")
        AC = p.sem("ac")
        PE = p.sem("pe")
        RMS = p.sem("rms")
        CC = p.sem("cc")
        ST = p.sem("st")
        WFG = p.sem("wfg")
        WFU = p.sem("wfu")
        WFP = p.sem("wfp")
        WFA = p.sem("wfa")
        QF = p.sem("qf")

        t_id = p.dma("sync", ident[:], ident_in[:])
        t_x = p.dma("sync", xT[:].rearrange("p (c t) -> p c t", t=TLOC),
                    x0T.rearrange("(c p) t -> p c t", p=128))
        t_lnw = p.dma("sync", lnw[:], lnw_in[:])
        t_o1 = p.dma("sync", onesf[:], ones_in[:])
        t_ob = p.dma("sync", onesb[:], onesb_in[:])
        t_offk = p.dma("sync", offk_sb[:], off_k_in[:])
        t_offv = p.dma("sync", offv_sb[:], off_v_in[:])
        t_consts = t_offv

        def av_(off, ncol):
            return arena[:, off:off + ncol]

        qw = [av_(0 + 2048 * i, 2048) for i in range(2)]
        qf = [av_(4096 + 2048 * i, 2048) for i in range(2)]
        qstage = [av_(8192 + 2048 * i, 2048) for i in range(2)]
        qrq = av_(12288, 2048)
        fwp = av_(16384, 16384)
        with contextlib.ExitStack() as qctx:
            qsb = lambda nm, shp, dt: qctx.enter_context(nc.sbuf_tensor(nm, shp, dt))
            qps = lambda nm, shp, dt: qctx.enter_context(nc.psum_tensor(nm, shp, dt))
            ps_mark_q = qctx.enter_context(nc.psum_tensor("ps_mark_q", [1, 8], F32))
            pe_mark = mk_pe_mark(ps_mark_q)
            qgs = qsb("qgs", [128, 32], F32)
            qscb = qsb("qscb", [128, 32], BF16)
            qscf = qsb("qscf", [128, 32], F32)
            qrcp = qsb("qrcp", [128, 32], F32)
            ps_tp = [qps(f"ps_tp{i}", [128, 2048], BF16) for i in range(2)]

            evac_toks = []
            store_toks = []
            tp_toks = []

            def quant_core(idx, wbuf, dst, ncols, ngrp, t_ld):
                v = "vector"
                p.efence(v, lambda e: e.tensor_reduce(
                    qgs[:, 0:ngrp], wbuf.rearrange("p (g k) -> p g k", k=64),
                    axis=AX.X, op=ALU.add, apply_absolute_value=True), waits=[t_ld])
                p.efence(v, lambda e: e.tensor_scalar(qscb[:, 0:ngrp], qgs[:, 0:ngrp],
                                                      1.0 / 64, 1e-8, ALU.mult, ALU.max))
                p.efence(v, lambda e: e.tensor_copy(qscf[:, 0:ngrp], qscb[:, 0:ngrp]))
                p.efence(v, lambda e: e.reciprocal(qrcp[:, 0:ngrp], qscf[:, 0:ngrp]))
                qr = qrq[:, 0:ncols]
                p.emit(v, lambda e: e.tensor_tensor(
                    qr.rearrange("p (g k) -> p g k", k=64), wbuf,
                    qrcp[:, 0:ngrp, None].broadcast_to([128, ngrp, 64]), ALU.mult))
                p.emit(v, lambda e: e.tensor_scalar_add(qr, qr, 192.0))
                p.emit(v, lambda e: e.tensor_scalar(qr, qr, 193.0, 191.0, ALU.min, ALU.max))
                p.emit(v, lambda e: e.tensor_scalar_add(qr, qr, -192.0))
                p.emit(v, lambda e: e.tensor_tensor(
                    dst.rearrange("p (g k) -> p g k", k=64), qr,
                    qscb[:, 0:ngrp, None].broadcast_to([128, ngrp, 64]), ALU.mult))
                p.emit(v, lambda e: e.tensor_sub(qr, dst, wbuf))
                p.emit(v, lambda e: e.tensor_add(dst, wbuf, qr), inc=QF)
                return p.tok(QF)

            for ddc in range(16):
                par = ddc % 2
                wbuf = qw[par]
                waits_ld = []
                if ddc >= 2:
                    waits_ld.append((QF, p.counts[QF] - 1))
                t_ld = p.dma("gpsimd", wbuf[:, 0:1024],
                             wp_raw[ddc * 128:(ddc + 1) * 128, :],
                             waits=waits_ld)
                quant_core(ddc, wbuf[:, 0:1024],
                           fwp[:, ddc * 1024:(ddc + 1) * 1024], 1024, 16, t_ld)
            t_fwp = p.tok(QF)
            if dbg:
                p.dma("sync", dbg_fwp[:], fwp[:, 0:2048], waits=[t_fwp], sem=ST)
            wstg = 0
            for hs in range(8):
                par = hs % 2
                stage, pst = qstage[par], ps_tp[par]
                wtp = [t_fwp]
                if len(evac_toks) >= 2:
                    wtp.append(evac_toks[-2])
                for ddc in range(16):
                    p.emit("tensor", lambda e, ddc=ddc, hs=hs, pst=pst: e.transpose(
                        pst[:, ddc * 128:(ddc + 1) * 128],
                        fwp[:, ddc * 1024 + hs * 128: ddc * 1024 + (hs + 1) * 128],
                        ident[:]), waits=wtp if ddc == 0 else [])
                pe_mark(inc=PE)
                t_tp = p.tok(PE)
                wev = [t_tp]
                if len(store_toks) >= 2:
                    wev.append(store_toks[-2])
                p.emit("scalar", lambda e, stage=stage, pst=pst: e.copy(stage, pst[:]),
                       waits=wev, inc=AC)
                evac_toks.append(p.tok(AC))
                if dbg and hs == 0:
                    p.dma("sync", dbg_st2[:], stage, waits=[evac_toks[-1]], sem=ST)
                store_toks.append(
                    p.dma("gpsimd", ag_p_in[hs * 128:(hs + 1) * 128, :], stage,
                          waits=[evac_toks[-1]]))
                wstg += 1
            t_p_written = p.tok(GLD)
            p.emit("gpsimd", lambda e: e.collective_compute(
                "AllGather", ALU.bypass, replica_groups=[list(range(NC))],
                ins=[ag_p_in[:]], outs=[ag_p[:]]), waits=[t_p_written], inc=CC)
            T_AG_P = p.tok(CC)
            for rc in range(16):
                par = rc % 2
                wbuf, fbuf, stage, pst = qw[par], qf[par], qstage[par], ps_tp[par]
                waits_ld = [(QF, p.counts[QF] - 1)]
                t_ld = p.dma("gpsimd", wbuf, gu_raw[rc * 128:(rc + 1) * 128, :],
                             waits=waits_ld)
                if rc >= 2:
                    p.wait("vector", tp_toks[rc - 2])
                t_f = quant_core(rc, wbuf, fbuf, 2048, 32, t_ld)
                if dbg and rc == 0:
                    p.dma("sync", dbg_f[:], fbuf, waits=[t_f], sem=ST)
                wtp = [t_f, t_id]
                if len(evac_toks) >= 2:
                    wtp.append(evac_toks[-2])
                for s in range(16):
                    p.emit("tensor", lambda e, s=s, fbuf=fbuf, pst=pst: e.transpose(
                        pst[:, s * 128:(s + 1) * 128], fbuf[:, s * 128:(s + 1) * 128],
                        ident[:]), waits=wtp if s == 0 else [])
                pe_mark(inc=PE)
                tp_toks.append(p.tok(PE))
                wev = [tp_toks[-1]]
                if len(store_toks) >= 2:
                    wev.append(store_toks[-2])
                p.emit("scalar", lambda e, stage=stage, pst=pst: e.copy(stage, pst[:]),
                       waits=wev, inc=AC)
                evac_toks.append(p.tok(AC))
                if dbg and rc == 0:
                    p.dma("sync", dbg_stage[:], stage, waits=[evac_toks[-1]], sem=ST)
                store_toks.append(
                    p.dma("gpsimd", ag_gu_in[rc * 128:(rc + 1) * 128, :], stage,
                          waits=[evac_toks[-1]] + ([p.tok(ST)] if (dbg and rc == 0) else [])))

            t_gu_written = p.tok(GLD)
            p.emit("gpsimd", lambda e: e.collective_compute(
                "AllGather", ALU.bypass, replica_groups=[list(range(NC))],
                ins=[ag_gu_in[:]], outs=[ag_gu[:]]), waits=[t_gu_written], inc=CC)
            T_AG_GU = p.tok(CC)

            t_quant_ac = evac_toks[-1]

        def ln_block(ps_A, ps_B, tt0, pre_pe):
            stat_toks = []
            for c in range(NCH):
                par = c % 2
                xb, sq = xb_s[par], sq_s[par]
                w0 = [t_x]
                if c >= 2:
                    w0.append(stat_toks[c - 2])
                p.emit("vector", lambda e, c=c, xb=xb: e.tensor_copy(
                    xb[:], xT[:, c * TLOC + tt0: c * TLOC + tt0 + TT]), waits=w0)
                p.emit("vector", lambda e, xb=xb, sq=sq: e.tensor_mul(sq[:], xb[:], xb[:]),
                       inc=DV)
                tdv = p.tok(DV)
                p.emit("tensor", lambda e, c=c, xb=xb: e.matmul(
                    ps_A[0:1, :], onesb[:], xb[:], start=(c == 0), stop=(c == NCH - 1)),
                    waits=[tdv, t_ob] + (pre_pe if c == 0 else []))
                p.emit("tensor", lambda e, c=c, sq=sq: e.matmul(
                    ps_B[0:1, :], onesb[:], sq[:], start=(c == 0), stop=(c == NCH - 1)),
                    inc=PE)
                stat_toks.append(p.tok(PE))
            t_stat = stat_toks[-1]
            v = "vector"
            p.emit(v, lambda e: e.tensor_copy(rows[0:1, 0:TT], ps_A[0:1, :]), waits=[t_stat])
            p.efence(v, lambda e: e.tensor_copy(rows[0:1, TT:2 * TT], ps_B[0:1, :]))
            p.emit(v, lambda e: e.tensor_scalar_mul(rows[0:1, 2 * TT:3 * TT], rows[0:1, 0:TT], 1.0 / D))
            p.efence(v, lambda e: e.tensor_scalar_mul(rows[0:1, 3 * TT:4 * TT], rows[0:1, TT:2 * TT], 1.0 / D))
            p.efence(v, lambda e: e.tensor_mul(rows[0:1, 4 * TT:5 * TT],
                                               rows[0:1, 2 * TT:3 * TT], rows[0:1, 2 * TT:3 * TT]))
            p.emit(v, lambda e: e.scalar_tensor_tensor(
                rows[0:1, 3 * TT:4 * TT], rows[0:1, 3 * TT:4 * TT], EPS_LN,
                rows[0:1, 4 * TT:5 * TT], ALU.add, ALU.subtract), inc=DV)
            t_var = p.tok(DV)
            p.emit("scalar", lambda e: e.activation(rows[0:1, 4 * TT:5 * TT],
                                                    rows[0:1, 3 * TT:4 * TT], AF.Ln,
                                                    bias=0.0, scale=1.0),
                   waits=[t_var], inc=AC)
            t_ln = p.tok(AC)
            p.emit(v, lambda e: e.tensor_scalar_mul(rows[0:1, 4 * TT:5 * TT],
                                                    rows[0:1, 4 * TT:5 * TT], -0.5),
                   waits=[t_ln], inc=DV)
            t_nl = p.tok(DV)
            p.emit("scalar", lambda e: e.activation(rows[0:1, 4 * TT:5 * TT],
                                                    rows[0:1, 4 * TT:5 * TT], AF.Exp,
                                                    bias=0.0, scale=1.0),
                   waits=[t_nl], inc=AC)
            t_rstd = p.tok(AC)
            p.emit(v, lambda e: e.scalar_tensor_tensor(
                rows[0:1, 5 * TT:6 * TT], rows[0:1, 2 * TT:3 * TT], -1.0,
                rows[0:1, 4 * TT:5 * TT], ALU.mult, ALU.mult), waits=[t_rstd], inc=DV)
            t_ab = p.tok(DV)
            p.emit("tensor", lambda e: e.matmul(ps_A[:, :], onesf[:], rows[0:1, 4 * TT:5 * TT],
                                                start=True, stop=True), waits=[t_ab, t_o1])
            p.emit("tensor", lambda e: e.matmul(ps_B[:, :], onesf[:], rows[0:1, 5 * TT:6 * TT],
                                                start=True, stop=True), inc=PE)
            t_bc = p.tok(PE)
            p.emit("scalar", lambda e: e.copy(bcA[:], ps_A[:, :]), waits=[t_bc])
            p.emit("scalar", lambda e: e.copy(bcB[:], ps_B[:, :]), inc=AC)
            return p.tok(AC)

        def ffn_call(tag, pre_pe_tok):
            hbuf = av_(0, NHC * TT)
            with contextlib.ExitStack() as fctx:
                fsb = lambda nm, shp, dt: fctx.enter_context(nc.sbuf_tensor(nm, shp, dt))
                fps = lambda nm, shp, dt: fctx.enter_context(nc.psum_tensor(nm, shp, dt))
                xln = scr16[:, 0:NCH * TT]
                wg_b = [fsb(f"wgb{tag}{i}", [128, 2048], BF16) for i in range(2)]
                wu_b = [fsb(f"wub{tag}{i}", [128, 2048], BF16) for i in range(2)]
                wp_b = [fsb(f"wpb{tag}{i}", [128, 2 * 768], BF16) for i in range(2)]
                sg_s = [fsb(f"sg{tag}{i}", [128, TT], BF16) for i in range(2)]
                ue_s = [fsb(f"ue{tag}{i}", [128, TT], BF16) for i in range(2)]
                binv = fsb(f"binv_{tag}", [128, TT], F32)
                ps_A = fps(f"psA{tag}", [128, TT], F32)
                ps_B = fps(f"psB{tag}", [128, TT], F32)
                ps_bank = [fps(f"psk{tag}{i}", [128, TT], F32) for i in range(6)]

                pre = [pre_pe_tok]
                t_tile_guard = pre_pe_tok
                for tt in range(NTT):
                    tt0 = tt * TT
                    t_bcs = ln_block(ps_A, ps_B, tt0, pre)
                    pre = []
                    for c in range(NCH):
                        dst = xln[:, c * TT:(c + 1) * TT]
                        xsl = xT[:, c * TLOC + tt0: c * TLOC + tt0 + TT]
                        p.emit("vector", lambda e, dst=dst, xsl=xsl: e.tensor_mul(dst, xsl, bcA[:]),
                               waits=[t_bcs])
                        p.emit("vector", lambda e, dst=dst: e.tensor_add(dst, dst, bcB[:]),
                               inc=(DV if c == NCH - 1 else None))
                    t_xln = p.tok(DV)

                    gate_toks = {}
                    up_toks = {}

                    def load_gu(hc):
                        wg = [T_AG_GU, (WFG, p.counts[WFG])]
                        wu = [(WFU, p.counts[WFU])]
                        if hc < 2 and t_tile_guard is not None:
                            wg.append(t_tile_guard)
                            wu.append(t_tile_guard)
                        gate_toks[hc] = p.dma(
                            "sync", wg_b[hc % 2][:],
                            ag_gu[hc * 128:(hc + 1) * 128, :], waits=wg, sem=WLD)
                        up_toks[hc] = p.dma(
                            "sync", wu_b[hc % 2][:],
                            ag_gu[(64 + hc) * 128:(65 + hc) * 128, :], waits=wu, sem=WLD)

                    load_gu(0)
                    load_gu(1)
                    for hc in range(NHC):
                        par = hc % 2
                        pg, pu = ps_bank[2 * par], ps_bank[2 * par + 1]
                        wgu = [gate_toks[hc], t_xln]
                        if hc < 2 and t_tile_guard is not None:
                            wgu.append(t_tile_guard)
                        if hc >= 2:
                            wgu.append((AC, p.counts[AC] - 1))
                        for dc in range(NCH):
                            p.emit("tensor", lambda e, dc=dc, pg=pg, par=par: e.matmul(
                                pg[:], wg_b[par][:, dc * 128:(dc + 1) * 128],
                                xln[:, dc * TT:(dc + 1) * TT],
                                start=(dc == 0), stop=(dc == NCH - 1)),
                                waits=wgu if dc == 0 else [],
                                inc=([(WFG, 1)] if dc == NCH - 1 else None))
                        for dc in range(NCH):
                            p.emit("tensor", lambda e, dc=dc, pu=pu, par=par: e.matmul(
                                pu[:], wu_b[par][:, dc * 128:(dc + 1) * 128],
                                xln[:, dc * TT:(dc + 1) * TT],
                                start=(dc == 0), stop=(dc == NCH - 1)),
                                waits=[up_toks[hc]] if dc == 0 else [],
                                inc=([(WFU, 1)] if dc == NCH - 1 else None))
                        t_gu_mm = p.tok(WFU)
                        if hc + 2 < NHC:
                            load_gu(hc + 2)
                        sg, ue = sg_s[par], ue_s[par]
                        wsil = [t_gu_mm]
                        if hc >= 2:
                            wsil.append((DV, p.counts[DV] - 1))
                        p.emit("scalar", lambda e, sg=sg, pg=pg: e.activation(
                            sg[:], pg[:], AF.Silu, bias=0.0, scale=1.0), waits=wsil)
                        p.emit("scalar", lambda e, ue=ue, pu=pu: e.copy(ue[:], pu[:]),
                               inc=AC)
                        t_sil = p.tok(AC)
                        hdst = hbuf[:, hc * TT:(hc + 1) * TT]
                        hsq = sq_s[par]
                        whm = [t_sil]
                        if hc >= 2:
                            whm.append((RMS, p.counts[RMS] - 1))
                        p.emit("vector", lambda e, sg=sg, ue=ue, hdst=hdst: e.tensor_mul(
                            hdst, sg[:], ue[:]), waits=whm)
                        p.emit("vector", lambda e, hdst=hdst, hsq=hsq: e.tensor_mul(
                            hsq[:], hdst, hdst), inc=DV)
                        t_hsq = p.tok(DV)
                        p.emit("tensor", lambda e, hc=hc, hsq=hsq: e.matmul(
                            ps_A[0:1, :], onesb[:], hsq[:],
                            start=(hc == 0), stop=(hc == NHC - 1)),
                            waits=[t_hsq], inc=RMS)
                    t_rms = p.tok(RMS)

                    v = "vector"
                    p.efence(v, lambda e: e.tensor_copy(rows[0:1, 0:TT], ps_A[0:1, :]),
                             waits=[t_rms])
                    p.emit(v, lambda e: e.tensor_scalar(rows[0:1, TT:2 * TT], rows[0:1, 0:TT],
                                                        1.0 / HID, EPS_F32, ALU.mult, ALU.add),
                           inc=DV)
                    t_ms = p.tok(DV)
                    p.emit("scalar", lambda e: e.activation(rows[0:1, 2 * TT:3 * TT],
                                                            rows[0:1, TT:2 * TT], AF.Ln,
                                                            bias=0.0, scale=1.0),
                           waits=[t_ms], inc=AC)
                    t_l = p.tok(AC)
                    p.emit(v, lambda e: e.tensor_scalar_mul(rows[0:1, 2 * TT:3 * TT],
                                                            rows[0:1, 2 * TT:3 * TT], -0.5),
                           waits=[t_l], inc=DV)
                    t_nl = p.tok(DV)
                    p.emit("scalar", lambda e: e.activation(rows[0:1, 3 * TT:4 * TT],
                                                            rows[0:1, 2 * TT:3 * TT], AF.Exp,
                                                            bias=0.0, scale=1.0),
                           waits=[t_nl], inc=AC)
                    t_inv = p.tok(AC)
                    p.emit(v, lambda e: e.tensor_scalar_mul(rows[0:1, 4 * TT:5 * TT],
                                                            rows[0:1, 3 * TT:4 * TT], 0.5),
                           waits=[t_inv], inc=DV)
                    t_bi = p.tok(DV)
                    p.emit("tensor", lambda e: e.matmul(ps_A[:, :], onesf[:],
                                                        rows[0:1, 4 * TT:5 * TT],
                                                        start=True, stop=True),
                           waits=[t_bi, t_o1], inc=PE)
                    t_biv = p.tok(PE)
                    p.emit("scalar", lambda e: e.copy(binv[:], ps_A[:, :]),
                           waits=[t_biv], inc=AC)
                    t_binv = p.tok(AC)

                    t_prev_evac = None
                    for gi, (d0, nd) in enumerate(DDG):
                        ncol = nd * 128
                        wp_toks = {}

                        def load_wp(gb, d0=d0, ncol=ncol):
                            wv_ = [T_AG_P, (WFP, p.counts[WFP])]
                            if gb < 2 and t_tile_guard is not None:
                                wv_.append(t_tile_guard)
                            wp_toks[gb] = p.dma(
                                "sync",
                                wp_b[gb % 2][:].rearrange("p (r c) -> p r c", c=768)[:, :, 0:ncol],
                                ag_p.rearrange("(r p) c -> p r c", p=128)[
                                    :, 2 * gb:2 * gb + 2, d0 * 128:d0 * 128 + ncol],
                                waits=wv_, sem=WLD)

                        load_wp(0)
                        load_wp(1)
                        for hc in range(NHC):
                            gb, sub = hc // 2, hc % 2
                            par = gb % 2
                            wpr = [wp_toks[gb]]
                            if hc == 0:
                                wpr.append((AC, p.counts[AC]))
                                if t_prev_evac is not None:
                                    wpr.append(t_prev_evac)
                            for di in range(nd):
                                last = (di == nd - 1 and sub == 1)
                                inc_ = [(WFP, 1)] if last else None
                                p.emit("tensor", lambda e, di=di, hc=hc, par=par, sub=sub: e.matmul(
                                    ps_bank[di][:],
                                    wp_b[par][:, sub * 768 + di * 128: sub * 768 + (di + 1) * 128],
                                    hbuf[:, hc * TT:(hc + 1) * TT],
                                    start=(hc == 0), stop=(hc == NHC - 1)),
                                    waits=wpr if di == 0 else [],
                                    inc=inc_)
                            if sub == 1 and gb + 2 < NHC // 2:
                                load_wp(gb + 2)
                        t_proj = p.tok(WFP)
                        for di in range(nd):
                            dd = d0 + di
                            xsl = xT[:, dd * TLOC + tt0: dd * TLOC + tt0 + TT]
                            p.emit("vector", lambda e, di=di, xsl=xsl: e.tensor_mul(
                                rtmp[:], ps_bank[di][:], binv[:]),
                                waits=[t_proj, t_binv])
                            p.emit("vector", lambda e, xsl=xsl: e.tensor_add(xsl, xsl, rtmp[:]),
                                   inc=(DV if di == nd - 1 else None))
                        t_prev_evac = p.tok(DV)
                    t_tile_guard = t_prev_evac
                return p.tok(DV)

        t_after = None
        if "ffn1" in parts:
            t_after = ffn_call("f1", t_quant_ac)

        if "attn" in parts:
            xnT = av_(0, NCH * TLOC)
            xl2 = av_(16384, NCH * TT)
            vTf = av_(24576, KVH * TLOC)
            aoT = av_(16384, NCH * TLOC)
            with contextlib.ExitStack() as actx:
                asb = lambda nm, shp, dt: actx.enter_context(nc.sbuf_tensor(nm, shp, dt))
                aps = lambda nm, shp, dt: actx.enter_context(nc.psum_tensor(nm, shp, dt))
                kT = asb("kT", [128, KVH * 1536], BF16)
                vA = asb("vA", [128, 12 * 512], BF16)
                qraw = asb("qraw", [128, TT], F32)
                ropeC2 = asb("ropeC2", [128, 1024], BF16)
                ropeS2 = asb("ropeS2", [128, 1024], BF16)
                pswap = asb("pswap_t", [128, 128], F32)
                wqa = [asb(f"wqa{i}", [128, NCH * 128], BF16) for i in range(2)]
                ngmx = asb("ngmx", [128, 1], F32)
                rsum = asb("rsum", [128, 1], F32)
                rrcp = asb("rrcp", [128, 1], F32)
                masks = scr16[:, 0:NQT * KWIN]
                qrot1 = scr16[:, 5120:6144]
                probs = scr16[:, 6144:6784]
                pmm = scr16[:, 6784:7424]
                pT1 = scr16[:, 7424:8064]

                t_mask = p.dma("sync", masks.rearrange("p (q k) -> p q k", k=KWIN),
                               mask_in.rearrange("q p k -> p q k"),
                               waits=[t_after])
                t_ropeC = p.dma("sync", ropeC2[:], ropeC_in[:])
                t_ropeS = p.dma("sync", ropeS2[:], ropeS_in[:])
                t_psw = p.dma("sync", pswap[:], pswap_in[:])
                ps_A2 = aps("psA2", [128, TT], F32)
                ps_B2 = aps("psB2", [128, TT], F32)
                ps_q = aps("psq", [128, TT], F32)
                ps_sc = aps("ps_sc", [128, KWIN], F32)
                ps_pT = aps("ps_pT", [128, KWIN], BF16)
                ps_av = aps("ps_av", [128, 128], F32)

                pre = [t_after]
                for tt in range(NTT):
                    tt0 = tt * TT
                    t_bcs = ln_block(ps_A2, ps_B2, tt0, pre)
                    pre = []
                    for c in range(NCH):
                        dst = xl2[:, c * TT:(c + 1) * TT]
                        xsl = xT[:, c * TLOC + tt0: c * TLOC + tt0 + TT]
                        p.emit("vector", lambda e, dst=dst, xsl=xsl: e.tensor_mul(dst, xsl, bcA[:]),
                               waits=[t_bcs])
                        p.emit("vector", lambda e, dst=dst: e.tensor_add(dst, dst, bcB[:]),
                               inc=(DV if c == NCH - 1 else None))
                    t_xl2 = p.tok(DV)
                    rms_toks = []
                    for c in range(NCH):
                        par = c % 2
                        sq = sq_s[par]
                        w0 = [t_xl2]
                        if c >= 2:
                            w0.append(rms_toks[c - 2])
                        p.emit("vector", lambda e, c=c, sq=sq: e.tensor_mul(
                            sq[:], xl2[:, c * TT:(c + 1) * TT], xl2[:, c * TT:(c + 1) * TT]),
                            waits=w0, inc=DV)
                        tdv = p.tok(DV)
                        p.emit("tensor", lambda e, c=c, sq=sq: e.matmul(
                            ps_A2[0:1, :], onesb[:], sq[:], start=(c == 0), stop=(c == NCH - 1)),
                            waits=[tdv], inc=PE)
                        rms_toks.append(p.tok(PE))
                    t_st2 = rms_toks[-1]
                    v = "vector"
                    p.efence(v, lambda e: e.tensor_copy(rows[0:1, 0:TT], ps_A2[0:1, :]),
                             waits=[t_st2])
                    p.emit(v, lambda e: e.tensor_scalar(rows[0:1, TT:2 * TT], rows[0:1, 0:TT],
                                                        1.0 / D, EPS_ATTN, ALU.mult, ALU.add),
                           inc=DV)
                    t_ms = p.tok(DV)
                    p.emit("scalar", lambda e: e.activation(rows[0:1, 2 * TT:3 * TT],
                                                            rows[0:1, TT:2 * TT], AF.Ln,
                                                            bias=0.0, scale=1.0),
                           waits=[t_ms], inc=AC)
                    t_l = p.tok(AC)
                    p.emit(v, lambda e: e.tensor_scalar_mul(rows[0:1, 2 * TT:3 * TT],
                                                            rows[0:1, 2 * TT:3 * TT], -0.5),
                           waits=[t_l], inc=DV)
                    t_nl = p.tok(DV)
                    p.emit("scalar", lambda e: e.activation(rows[0:1, 3 * TT:4 * TT],
                                                            rows[0:1, 2 * TT:3 * TT], AF.Exp,
                                                            bias=0.0, scale=1.0),
                           waits=[t_nl], inc=AC)
                    t_r2 = p.tok(AC)
                    p.emit("tensor", lambda e: e.matmul(ps_A2[:, :], onesf[:],
                                                        rows[0:1, 3 * TT:4 * TT],
                                                        start=True, stop=True),
                           waits=[t_r2, t_o1], inc=PE)
                    t_b2 = p.tok(PE)
                    p.emit("scalar", lambda e: e.copy(bcA[:], ps_A2[:, :]), waits=[t_b2],
                           inc=AC)
                    t_bc2 = p.tok(AC)
                    for c in range(NCH):
                        dst = xnT[:, c * TLOC + tt0: c * TLOC + tt0 + TT]
                        p.emit(v, lambda e, c=c, dst=dst: e.tensor_mul(
                            dst, xl2[:, c * TT:(c + 1) * TT], bcA[:]), waits=[t_bc2])
                        p.emit(v, lambda e, c=c, dst=dst: e.tensor_scalar_mul(
                            dst, dst, lnw[:, c:c + 1]),
                            waits=[t_lnw], inc=(DV if c == NCH - 1 else None))
                t_xn = p.tok(DV)

                nuse = [0]

                def wqa_load(src_ap, waits_extra=()):
                    par = nuse[0] % 2
                    w = list(waits_extra)
                    if nuse[0] < 2:
                        w.append(t_xn)
                    if nuse[0] >= 2:
                        w.append((WFA, p.counts[WFA] - 1))
                    tok = p.dma("gpsimd", wqa[par][:].rearrange("p (c j) -> p c j", j=128),
                                src_ap, waits=w)
                    nuse[0] += 1
                    return par, tok

                t_qevac_prev = None
                t_rope_prev = None
                kvhs = range(KVH) if "nokv" not in parts else range(0)
                for kvh in kvhs:
                    par, t_wk = wqa_load(
                        wk_in.rearrange("(c p) j -> p c j", p=128)[:, :, kvh * 128:(kvh + 1) * 128])
                    base = kvh * 1536 + 512
                    for th in range(2):
                        for dc in range(NCH):
                            last = dc == NCH - 1
                            p.emit("tensor", lambda e, dc=dc, th=th, par=par: e.matmul(
                                ps_q[:], wqa[par][:, dc * 128:(dc + 1) * 128],
                                xnT[:, dc * TLOC + th * TT: dc * TLOC + (th + 1) * TT],
                                start=(dc == 0), stop=(dc == NCH - 1)),
                                waits=([t_wk, t_xn] +
                                       ([(AC, p.counts[AC])] if (th > 0 or kvh > 0) else []))
                                if dc == 0 else [],
                                inc=([(WFA, 1)] if (last and th == 1) else
                                     ([(PE, 1)] if last else None)))
                        t_kmm = p.tok(PE) if th == 0 else p.tok(WFA)
                        p.emit("scalar", lambda e: e.copy(qraw[:], ps_q[:]),
                               waits=[t_kmm if th == 0 else (WFA, p.counts[WFA]),
                                      (DV, p.counts[DV])], inc=AC)
                        t_ke = p.tok(AC)
                        o = th * TT
                        dstk = kT[:, base + o:base + o + TT]
                        if "norope" in parts:
                            p.emit(v, lambda e, dstk=dstk: e.tensor_copy(dstk, qraw[:]),
                                   waits=[t_ke], inc=DV)
                        else:
                            p.emit("tensor", lambda e: e.matmul(
                                ps_sc[:, 0:TT], pswap[:], qraw[:], start=True, stop=True),
                                waits=[t_ke, t_psw, (DV, p.counts[DV])], inc=PE)
                            t_sw = p.tok(PE)
                            p.emit(v, lambda e, o=o: e.scalar_tensor_tensor(
                                rtmp[:], qraw[:], 1.0, ropeC2[:, o:o + TT], ALU.mult, ALU.mult),
                                waits=[t_ropeC, t_ropeS, t_ke])
                            p.emit(v, lambda e, dstk=dstk, o=o: e.scalar_tensor_tensor(
                                dstk, ps_sc[:, 0:TT], 1.0, ropeS2[:, o:o + TT], ALU.mult, ALU.mult),
                                waits=[t_sw])
                            p.emit(v, lambda e, dstk=dstk: e.tensor_add(dstk, dstk, rtmp[:]),
                                   inc=DV)
                t_krope = p.tok(DV)
                for kvh in (kvhs if "nov" not in parts else range(0)):
                    par, t_wv = wqa_load(
                        wv_in.rearrange("(c p) j -> p c j", p=128)[:, :, kvh * 128:(kvh + 1) * 128],
                        waits_extra=[(DV, p.counts[DV])])
                    for th in range(2):
                        for dc in range(NCH):
                            last = dc == NCH - 1
                            p.emit("tensor", lambda e, dc=dc, th=th, par=par: e.matmul(
                                ps_q[:], wqa[par][:, dc * 128:(dc + 1) * 128],
                                xnT[:, dc * TLOC + th * TT: dc * TLOC + (th + 1) * TT],
                                start=(dc == 0), stop=(dc == NCH - 1)),
                                waits=[t_wv, (AC, p.counts[AC])] if dc == 0 else [],
                                inc=([(WFA, 1)] if (last and th == 1) else
                                     ([(PE, 1)] if last else None)))
                        t_vmm = p.tok(PE) if th == 0 else p.tok(WFA)
                        dstv = vTf[:, kvh * TLOC + th * TT: kvh * TLOC + (th + 1) * TT]
                        p.emit("scalar", lambda e, dstv=dstv, th=th: e.copy(dstv, ps_q[:]),
                               waits=[t_vmm if th == 0 else (WFA, p.counts[WFA])], inc=AC)
                t_vf = p.tok(AC)
                tcs = range(8) if ("nokv" not in parts and "nov" not in parts and "novt" not in parts) else range(0)
                t_vev = None
                for tc in tcs:
                    for kvh in range(KVH):
                        wtt = [t_vf, t_id]
                        if t_vev is not None:
                            wtt.append(t_vev)
                        p.emit("tensor", lambda e, tc=tc, kvh=kvh: e.transpose(
                            ps_pT[:, kvh * 128:(kvh + 1) * 128],
                            vTf[:, kvh * TLOC + tc * 128: kvh * TLOC + (tc + 1) * 128],
                            ident[:]), waits=wtt if kvh == 0 else [],
                            inc=(PE if kvh == KVH - 1 else None))
                    t_tpv = p.tok(PE)
                    p.emit("scalar", lambda e, tc=tc: e.copy(
                        vA[:, (4 + tc) * 512: (5 + tc) * 512], ps_pT[:, 0:512]),
                        waits=[t_tpv], inc=AC)
                    t_vev = p.tok(AC)
                t_vA = p.tok(AC)

                do_xch = "noxch" not in parts
                if do_xch:
                    p.dma("gpsimd", kv_in[0:512, :].rearrange("(c p) t -> p c t", p=128),
                          kT[:].rearrange("p (c t) -> p c t", t=1536)[:, :, 1024:1536],
                          waits=[t_krope])
                    p.dma("gpsimd", kv_in[512:1024, :].rearrange("(c p) t -> p c t", p=128),
                          vA[:].rearrange("p (c t) -> p c t", t=512)[:, 8:12, :],
                          waits=[t_vA])
                    t_kvw = p.tok(GLD)
                    p.emit("gpsimd", lambda e: e.collective_compute(
                        "AllGather", ALU.bypass, replica_groups=[list(range(NC))],
                        ins=[kv_in[:]], outs=[kv_out[:]]), waits=[t_kvw], inc=CC)
                    t_agkv = p.tok(CC)

                def rb_k(e):
                    if "noro" in parts:
                        src = kv_out[0:512, :]
                    else:
                        from concourse.expressions import RuntimeValue
                        with e.register("offk") as r:
                            e.reg_load(r, offk_sb[0:1, 0:1])
                            src = kv_out[bass.ds(RuntimeValue(r), 512), :]
                    return e.dma_start(
                        out=kT[:].rearrange("p (c t) -> p c t", t=1536)[:, :, 0:512],
                        in_=src.rearrange("(c p) t -> p c t", p=128))
                if do_xch:
                    p.emit("gpsimd", rb_k, waits=[t_agkv, t_offk], inc=[(GLD, 16)])

                def rb_v(e):
                    if "noro" in parts:
                        src = kv_out[512:1024, :]
                    else:
                        from concourse.expressions import RuntimeValue
                        with e.register("offv") as r:
                            e.reg_load(r, offv_sb[0:1, 0:1])
                            src = kv_out[bass.ds(RuntimeValue(r), 512), :]
                    return e.dma_start(
                        out=vA[:].rearrange("p (c t) -> p c t", t=512)[:, 0:4, :],
                        in_=src.rearrange("(c p) t -> p c t", p=128))
                if do_xch:
                    p.emit("gpsimd", rb_v, waits=[t_offv], inc=[(GLD, 16)])
                t_halo = p.tok(GLD)

                if dbg:
                    p.dma("sync", dbg_xn[:], xnT, waits=[t_xn], sem=ST)
                    p.dma("sync", dbg_kT[:], kT[:], waits=[t_halo, t_krope], sem=ST)
                t_ao_toks = []
                t_qrope_prev = None
                heads = range(H) if "nohead" not in parts else range(0)
                for h in heads:
                    kvh = h // 4
                    par, t_wq = wqa_load(
                        wq_in.rearrange("(c p) j -> p c j", p=128)[:, :, h * 128:(h + 1) * 128])
                    qv = qrot1
                    for th in range(2):
                        for dc in range(NCH):
                            last = dc == NCH - 1
                            p.emit("tensor", lambda e, dc=dc, th=th, par=par: e.matmul(
                                ps_q[:], wqa[par][:, dc * 128:(dc + 1) * 128],
                                xnT[:, dc * TLOC + th * TT: dc * TLOC + (th + 1) * TT],
                                start=(dc == 0), stop=(dc == NCH - 1)),
                                waits=[t_wq, t_xn, (AC, p.counts[AC])] if dc == 0 else [],
                                inc=([(WFA, 1)] if (last and th == 1) else
                                     ([(PE, 1)] if last else None)))
                        t_qmm = p.tok(PE) if th == 0 else p.tok(WFA)
                        p.emit("scalar", lambda e: e.copy(qraw[:], ps_q[:]),
                               waits=[t_qmm if th == 0 else (WFA, p.counts[WFA]),
                                      (DV, p.counts[DV])], inc=AC)
                        t_qe = p.tok(AC)
                        o = th * TT
                        p.emit("tensor", lambda e: e.matmul(
                            ps_sc[:, 0:TT], pswap[:], qraw[:], start=True, stop=True),
                            waits=[t_qe, t_psw, (DV, p.counts[DV]),
                                   (AC, p.counts[AC])], inc=PE)
                        t_sw = p.tok(PE)
                        dstq = qv[:, o:o + TT]
                        p.emit(v, lambda e, o=o: e.scalar_tensor_tensor(
                            rtmp[:], qraw[:], SCALE, ropeC2[:, o:o + TT], ALU.mult, ALU.mult),
                            waits=[t_ropeC, t_ropeS, t_qe])
                        p.emit(v, lambda e, dstq=dstq, o=o: e.scalar_tensor_tensor(
                            dstq, ps_sc[:, 0:TT], SCALE, ropeS2[:, o:o + TT], ALU.mult, ALU.mult),
                            waits=[t_sw])
                        p.emit(v, lambda e, dstq=dstq: e.tensor_add(dstq, dstq, rtmp[:]),
                               inc=DV)
                    t_qrope_prev = p.tok(DV)
                    t_qrope = t_qrope_prev

                    t_exp_prev = None
                    t_ave_prev = None
                    t_pTe_prev = None
                    t_tp_last = None
                    for qt in range(NQT):
                        kbase = kvh * 1536 + qt * 128
                        wqk = [t_qrope, t_halo]
                        if t_exp_prev is not None:
                            wqk.append(t_exp_prev)
                        p.emit("tensor", lambda e, kbase=kbase, qv=qv, qt=qt: e.matmul(
                            ps_sc[:, 0:512], qv[:, qt * 128:(qt + 1) * 128],
                            kT[:, kbase:kbase + 512], start=True, stop=True), waits=wqk)
                        p.emit("tensor", lambda e, kbase=kbase, qv=qv, qt=qt: e.matmul(
                            ps_sc[:, 512:640], qv[:, qt * 128:(qt + 1) * 128],
                            kT[:, kbase + 512:kbase + 640], start=True, stop=True), inc=PE)
                        t_sc = p.tok(PE)
                        p.emit(v, lambda e: e.reduce_max(ngmx[:], ps_sc[:], axis=AX.X,
                                                         negate=True), waits=[t_sc], inc=DV)
                        t_ng = p.tok(DV)
                        p.emit("scalar", lambda e: e.activation(
                            probs[:], ps_sc[:], AF.Exp, bias=ngmx[:, 0:1], scale=1.0),
                            waits=[t_ng], inc=AC)
                        t_exp_prev = p.tok(AC)
                        p.emit(v, lambda e, qt=qt: e.scalar_tensor_tensor(
                            pmm[:], probs[:], 1.0, masks[:, qt * KWIN:(qt + 1) * KWIN],
                            ALU.mult, ALU.mult, accum_out=rsum[:]),
                            waits=[t_exp_prev, t_mask, t_tp_last], inc=FEN)
                        p.emit(v, lambda e: None, waits=[p.tok(FEN)])
                        p.efence(v, lambda e: e.reciprocal(rrcp[:], rsum[:]))
                        p.emit(v, lambda e: e.tensor_scalar_mul(pmm[:], pmm[:], rrcp[:, 0:1]),
                               inc=DV)
                        t_pn = p.tok(DV)
                        wtp = [t_pn]
                        if t_pTe_prev is not None:
                            wtp.append(t_pTe_prev)
                        for c in range(5):
                            p.emit("tensor", lambda e, c=c: e.transpose(
                                ps_pT[:, c * 128:(c + 1) * 128], pmm[:, c * 128:(c + 1) * 128],
                                ident[:]), waits=wtp if c == 0 else [],
                                inc=(PE if c == 4 else None))
                        t_tp = p.tok(PE)
                        t_tp_last = t_tp
                        pT = pT1
                        p.emit("scalar", lambda e, pT=pT: e.copy(pT[:], ps_pT[:]),
                               waits=[t_tp], inc=AC)
                        t_pTe_prev = p.tok(AC)
                        wav = [t_pTe_prev, t_vA]
                        if t_ave_prev is not None:
                            wav.append(t_ave_prev)
                        for c in range(5):
                            vseg = (qt + c) * 512 + kvh * 128
                            p.emit("tensor", lambda e, c=c, vseg=vseg, pT=pT: e.matmul(
                                ps_av[:], vA[:, vseg:vseg + 128], pT[:, c * 128:(c + 1) * 128],
                                start=(c == 0), stop=(c == 4)),
                                waits=wav if c == 0 else [],
                                inc=(PE if c == 4 else None))
                        t_avm = p.tok(PE)
                        p.emit("scalar", lambda e, h=h, qt=qt: e.copy(
                            aoT[:, h * TLOC + qt * 128: h * TLOC + (qt + 1) * 128],
                            ps_av[:]), waits=[t_avm], inc=AC)
                        t_ave_prev = p.tok(AC)
                t_ao = p.tok(AC)
                if dbg:
                    p.dma("sync", dbg_qv[:], qrot1, waits=[t_qrope_prev], sem=ST)
                    p.dma("sync", dbg_ao[:], aoT, waits=[t_ao], sem=ST)

                dds = range(NCH) if "nowo" not in parts else range(0)
                for dd in dds:
                    par, t_wo = wqa_load(
                        wo_in.rearrange("(c p) j -> p c j", p=128)[:, :, dd * 128:(dd + 1) * 128])
                    for th in range(2):
                        for oc in range(NCH):
                            last = oc == NCH - 1
                            p.emit("tensor", lambda e, oc=oc, th=th, par=par: e.matmul(
                                ps_q[:], wqa[par][:, oc * 128:(oc + 1) * 128],
                                aoT[:, oc * TLOC + th * TT: oc * TLOC + (th + 1) * TT],
                                start=(oc == 0), stop=(oc == NCH - 1)),
                                waits=[t_wo, t_ao, (DV, p.counts[DV])] if oc == 0 else [],
                                inc=([(WFA, 1)] if (last and th == 1) else
                                     ([(PE, 1)] if last else None)))
                        t_womm = p.tok(PE) if th == 0 else p.tok(WFA)
                        xsl = xT[:, dd * TLOC + th * TT: dd * TLOC + (th + 1) * TT]
                        p.emit("vector", lambda e, xsl=xsl: e.tensor_add(xsl, xsl, ps_q[:]),
                               waits=[t_womm if th == 0 else (WFA, p.counts[WFA])],
                               inc=DV)
                t_after = p.tok(DV)
                if "nowo" in parts:
                    t_after = t_xn
                p.wait("tensor", t_after)

        if "ffn2" in parts:
            t_after = ffn_call("f2", t_after)

        if dbg:
            p.dma("sync", dbg_gu[:], ag_gu[0:512, :], waits=[T_AG_GU], sem=ST)
            p.dma("sync", dbg_p[:], ag_p[0:256, :], waits=[T_AG_P], sem=ST)
            p.dma("sync", dbg_xln[:], scr16[:, 0:NCH * TT], waits=[p.tok(DV)], sem=ST)
            p.dma("sync", dbg_h[:], arena[:, 0:4 * TT], waits=[p.tok(DV)], sem=ST)
        t_done = [p.tok(DV), p.tok(AC)]
        p.emit("sync", lambda e: e.dma_start(
            out=out_d.rearrange("(c p) t -> p c t", p=128),
            in_=xT[:].rearrange("p (c t) -> p c t", t=TLOC)),
            waits=t_done, inc=[(ST, 16)])
        p.emit("sync", lambda e: None, waits=[p.tok(ST)])

        block = ctx.enter_context(nc.Block())
        p.replay(block)
    return nc


def _perm_cols(n_heads, hd=128):
    idx = []
    for hb in range(n_heads):
        base = hb * hd
        idx.extend([base + 2 * i for i in range(hd // 2)])
        idx.extend([base + 2 * i + 1 for i in range(hd // 2)])
    return np.array(idx)


def _host_inputs(L0_attn_x, L0_attn_ln_w, L0_attn_wq, L0_attn_wk, L0_attn_wv,
                 L0_attn_wo, L0_ffn_w_gate_up, L0_ffn_w_proj):
    x = np.asarray(L0_attn_x, dtype=np.float32)
    lnw_v = np.asarray(L0_attn_ln_w, dtype=np.float32)
    wq = np.ascontiguousarray(np.asarray(L0_attn_wq, dtype=np.float32))
    wk = np.ascontiguousarray(np.asarray(L0_attn_wk, dtype=np.float32))
    wv = np.ascontiguousarray(np.asarray(L0_attn_wv, dtype=np.float32))
    wo = np.ascontiguousarray(np.asarray(L0_attn_wo, dtype=np.float32))
    wgu = np.asarray(L0_ffn_w_gate_up, dtype=np.float32)
    wp = np.asarray(L0_ffn_w_proj, dtype=np.float32)

    ident = np.eye(128, dtype=bf)
    onesf = np.ones((1, 128), dtype=np.float32)
    onesb = np.ones((128, 1), dtype=bf)
    lnw_t = np.ascontiguousarray(lnw_v.reshape(NCH, 128).T)
    inv_freq = (1.0 / 10000.0 ** (np.arange(0, HD, 2, dtype=np.float32) / HD))
    pswap_m = np.zeros((128, 128), dtype=np.float32)
    for i in range(0, 128, 2):
        pswap_m[i, i + 1] = 1.0
        pswap_m[i + 1, i] = 1.0

    in_maps = []
    for r in range(NC):
        b, half = r // 2, r % 2
        x0T = np.ascontiguousarray(x[b, half * TLOC:(half + 1) * TLOC, :].T)
        pos = np.arange(half * TLOC, half * TLOC + TLOC, dtype=np.float32)
        ang = pos[None, :] * inv_freq[:, None]
        cosv = np.cos(ang).astype(np.float32)
        sinv = np.sin(ang).astype(np.float32)
        ropeC = np.empty((128, TLOC), dtype=np.float32)
        ropeC[0::2] = cosv
        ropeC[1::2] = cosv
        ropeS = np.empty((128, TLOC), dtype=np.float32)
        ropeS[0::2] = -sinv
        ropeS[1::2] = sinv
        ropeC = np.ascontiguousarray(ropeC.astype(bf))
        ropeS = np.ascontiguousarray(ropeS.astype(bf))
        qts = np.arange(NQT)
        i_ = np.arange(128)
        col = np.arange(KWIN)
        qpos = half * TLOC + qts[:, None, None] * 128 + i_[None, :, None]
        kpos = half * TLOC + qts[:, None, None] * 128 + col[None, None, :] - 512
        valid = (kpos >= qpos - 512) & (kpos <= qpos) & (kpos >= 0)
        maskmul = np.ascontiguousarray(valid.astype(bf))
        partner = r ^ 1
        in_maps.append({
            "x0T": x0T,
            "wq": wq, "wk": wk, "wv": wv, "wo": wo,
            "gu_raw": np.ascontiguousarray(wgu[r * (HID2 // NC):(r + 1) * (HID2 // NC), :]),
            "wp_raw": np.ascontiguousarray(wp[:, r * (HID // NC):(r + 1) * (HID // NC)]),
            "lnw": lnw_t,
            "ident": ident, "onesf": onesf, "onesb": onesb,
            "ropeC": ropeC, "ropeS": ropeS, "pswap": pswap_m,
            "maskmul": maskmul,
            "off_k": np.array([[partner * 1024]], dtype=np.int32),
            "off_v": np.array([[partner * 1024 + 512]], dtype=np.int32),
        })
    return in_maps


def kernel(L0_attn_x, L0_attn_ln_w, L0_attn_wq, L0_attn_wk, L0_attn_wv,
           L0_attn_wo, L0_ffn_w_gate_up, L0_ffn_w_proj):
    in_maps = _host_inputs(L0_attn_x, L0_attn_ln_w, L0_attn_wq, L0_attn_wk,
                           L0_attn_wv, L0_attn_wo, L0_ffn_w_gate_up, L0_ffn_w_proj)
    nc = build_program()
    res = run_bass_kernel_spmd(nc, in_maps, core_ids=list(range(NC)))
    out = np.empty((B, S, D), dtype=np.float32)
    for r in range(NC):
        b, half = r // 2, r % 2
        out[b, half * TLOC:(half + 1) * TLOC, :] = res.results[r]["out"].T
    return out
